# version 09
# baseline (speedup 1.0000x reference)
import contextlib
import numpy as np
import ml_dtypes
import concourse.bass as bass
import concourse.mybir as mybir
from concourse.bass_utils import run_bass_kernel_spmd

F32 = mybir.dt.float32
BF16 = mybir.dt.bfloat16
AF = mybir.ActivationFunctionType
ALU = mybir.AluOpType
AX = mybir.AxisListType

D = 1024
T = 2048
SEG = 512
NSEG = 4
H = 16
DK = 64
E = 8
HID = 2048
CH = 8
TC = 16
NCORE = 8
DEFER_UP = True

ENGINES = ("tensor", "vector", "scalar", "gpsimd", "sync")


class Em:

    def __init__(self):
        self.q = {e: [] for e in ENGINES}
        self.cnt = {s: 0 for s in ("sp", "sv", "sa", "ss", "sx", "sq", "sgc", "sg1", "sg2", "so", "so2")}

    def op(self, eng, fn, waits=(), inc=None, amt=1):
        flat = []
        for w in waits:
            if w is None:
                continue
            if isinstance(w, list):
                flat.extend(w)
            else:
                flat.append(w)
        ws = tuple(flat)
        tok = None
        if inc is not None:
            self.cnt[inc] += amt
            tok = (inc, self.cnt[inc])
        self.q[eng].append((fn, ws, inc, amt))
        return tok

    def replay_engine(self, eng, e, sems):
        last = {}
        for fn, ws, inc, amt in self.q[eng]:
            for s, v in ws:
                if last.get(s, 0) >= v:
                    continue
                last[s] = v
                e.wait_ge(sems[s], v)
            ins = fn(e)
            if inc is not None:
                ins.then_inc(sems[inc], amt)


def tmax(*toks):
    best = {}
    for t in toks:
        if t is None:
            continue
        items = t if isinstance(t, list) else [t]
        for s, v in items:
            if best.get(s, 0) < v:
                best[s] = v
    out = [(s, v) for s, v in best.items()]
    return out


def _mem_ap(ps_mem, h):
    p0 = 64 * (h % 2)
    f0 = 64 * (h // 2)
    return ps_mem[p0:p0 + 64, f0:f0 + 64]


def build(debug=None):
    import os as _os
    _lvl = int(_os.environ.get("P1B_LVL", "9"))
    _ns = int(_os.environ.get("ATTN_NS", str(NSEG)))
    _noqsw = bool(_os.environ.get("NOQSW"))
    nc = bass.Bass("TRN2", target_bir_lowering=False)

    x_h = nc.dram_tensor("x", [T, D], F32, kind="ExternalInput").ap()
    wqt_h = nc.dram_tensor("wqt", [128, CH * D], F32, kind="ExternalInput").ap()
    wkt_h = nc.dram_tensor("wkt", [128, CH * D], F32, kind="ExternalInput").ap()
    wvt_h = nc.dram_tensor("wvt", [128, CH * D], F32, kind="ExternalInput").ap()
    wot_h = nc.dram_tensor("wot", [128, CH * D], F32, kind="ExternalInput").ap()
    betab_h = nc.dram_tensor("betab", [128, H * DK], F32, kind="ExternalInput").ap()
    wr_h = nc.dram_tensor("wr", [128, CH * E], F32, kind="ExternalInput").ap()
    br_h = nc.dram_tensor("br", [E, 1], F32, kind="ExternalInput").ap()
    w1_h = nc.dram_tensor("w1", [E, D, HID], F32, kind="ExternalInput").ap()
    b1t_h = nc.dram_tensor("b1t", [128, 16 * E], F32, kind="ExternalInput").ap()
    w2_h = nc.dram_tensor("w2", [E, HID, D], F32, kind="ExternalInput").ap()
    b2_h = nc.dram_tensor("b2", [E, D], F32, kind="ExternalInput").ap()
    ident_h = nc.dram_tensor("ident", [128, 128], F32, kind="ExternalInput").ap()
    oh_h = nc.dram_tensor("oh", [E, E * 128], F32, kind="ExternalInput").ap()
    y_h = nc.dram_tensor("y", [T, D], F32, kind="ExternalOutput").ap()

    dbg = {}
    if debug in ("p1", "p1a", "p1b"):
        dbg["xnbf"] = nc.dram_tensor("d_xnbf", [128, CH * T], BF16, kind="ExternalOutput").ap()
        if debug == "p1":
            dbg["gates"] = nc.dram_tensor("d_gates", [E, T], BF16, kind="ExternalOutput").ap()
        if debug == "p1b":
            dbg["lg"] = nc.dram_tensor("d_lg", [E, T], F32, kind="ExternalOutput").ap()
    if debug == "attn":
        dbg["oacc"] = nc.dram_tensor("d_oacc", [128, CH * T], BF16, kind="ExternalOutput").ap()

    em = Em()
    root = contextlib.ExitStack()

    def sb(st, name, shape, dtype, side=None):
        return st.enter_context(nc.sbuf_tensor(name, shape, dtype, side=side))

    def pt(st, name, shape, dtype=F32):
        return st.enter_context(nc.psum_tensor(name, shape, dtype))

    xn_bf = sb(root, "xn_bf", [128, CH, T], BF16)
    out_acc = sb(root, "out_acc", [128, CH, T], BF16)
    gates_fm = sb(root, "gates_fm", [E, T], BF16)
    mem_sb = sb(root, "mem_sb", [128, H, 72], BF16)
    idf = sb(root, "idf", [128, 128], F32)
    idb = sb(root, "idb", [128, 128], BF16)
    omgf = sb(root, "omgf", [128, H * DK], BF16)
    onesr = sb(root, "onesr", [33, 128], BF16)
    onesc = sb(root, "onesc", [128, 1], BF16)

    em.op("sync", lambda s: s.dma_start(idf[:, :], ident_h), inc="ss", amt=16)
    t_idb = em.op("gpsimd", lambda g: g.dma_start(idb[:, :], ident_h), inc="sgc", amt=16)
    em.op("vector", lambda v: v.memset(onesr[:, :], 1.0))
    em.op("vector", lambda v: v.memset(onesc[:, :], 1.0))
    t_mones = em.op("vector", lambda v: v.memset(mem_sb[:, :, 64:65], 1.0), inc="sv")

    p1 = contextlib.ExitStack()
    x_tm = sb(p1, "x_tm", [128, TC, D], F32)
    wrsb = sb(p1, "wrsb", [128, CH, E], F32)
    brc = sb(p1, "brc", [E, 1], F32)
    betf = sb(p1, "betf", [128, H * DK], F32)
    stats = sb(p1, "stats", [128, 8, TC], F32)
    st2 = sb(p1, "st2", [128, 4, TC], F32)
    colbuf = sb(p1, "colbuf", [128, CH, 512], F32)
    logits_sb = sb(p1, "logits_sb", [E, T], F32)
    sqscr = sb(p1, "sqscr", [128, 1, D], F32)
    msk = sb(p1, "msk", [128, TC, E], F32)
    exps = sb(p1, "exps", [128, TC, E], F32)
    gtm = sb(p1, "gtm", [128, TC, E], F32)

    p1p = contextlib.ExitStack()
    ps_tr = [pt(p1p, "ps_tr0", [128, 512]), pt(p1p, "ps_tr1", [128, 512])]
    ps_lg = [pt(p1p, f"ps_lg{i}", [E, 512]) for i in range(4)]
    ps_lt = pt(p1p, "ps_lt", [128, TC, E])
    ps_gt = ps_tr

    em.op("sync", lambda s: s.dma_start(wrsb[:, :, :], wr_h.rearrange("p (c e) -> p c e", c=CH)), inc="ss", amt=16)
    em.op("sync", lambda s: s.dma_start(brc[:, :], br_h), inc="ss", amt=16)
    em.op("sync", lambda s: s.dma_start(betf[:, :], betab_h), inc="ss", amt=16)
    t_consts = ("ss", em.cnt["ss"])
    t_wr = t_br = t_bet = t_idf = t_consts
    t_sig = em.op("scalar", lambda a: a.activation(betf[:, :], betf[:, :], AF.Sigmoid), waits=[t_consts], inc="sa")
    t_omg = em.op("vector", lambda v: v.tensor_scalar(omgf[:, :], betf[:, :], -1.0, 1.0, ALU.mult, ALU.add),
                  waits=[t_sig], inc="sv")
    x_ap = x_h.rearrange("(tc p) d -> p tc d", p=128)
    for q in range(4):
        em.op("sync", lambda s, q=q: s.dma_start(x_tm[:, 4 * q:4 * q + 4, :], x_ap[:, 4 * q:4 * q + 4, :]),
              inc="sx", amt=16)
    t_xall = ("sx", em.cnt["sx"])
    t_x = [t_xall] * 4

    t_s1 = t_sqa = None
    sq_tok = [None, None]
    for tc in range(TC):
        t_s1 = em.op("vector", lambda v, tc=tc: v.reduce_sum(stats[:, 0, tc:tc + 1], x_tm[:, tc, :], axis=AX.X),
                     waits=[t_x[tc // 4]], inc="sv")
        t_sqa = em.op("scalar", lambda a, tc=tc: a.activation(sqscr[:, 0, :], x_tm[:, tc, :], AF.Square,
                                                              accum_out=stats[:, 1, tc:tc + 1]),
                      waits=tmax(t_x[tc // 4], sq_tok[0]), inc="sa")
        sq_tok[0] = t_sqa
    nmean, gm2, vpe, rstd, nmr, sd = (stats[:, i, :] for i in range(2, 8))
    t_a = em.op("vector", lambda v: v.tensor_scalar_mul(nmean, stats[:, 0, :], -1.0 / D), waits=[t_s1], inc="sv")
    t_b = em.op("vector", lambda v: v.tensor_mul(gm2, nmean, nmean), waits=[t_a], inc="sv")
    t_c = em.op("vector", lambda v: v.tensor_scalar_mul(vpe, stats[:, 1, :], 1.0 / D), waits=[t_sqa], inc="sv")
    t_d = em.op("vector", lambda v: v.tensor_sub(vpe, vpe, gm2), waits=tmax(t_b, t_c), inc="sv")
    t_vpe = em.op("vector", lambda v: v.tensor_scalar_add(vpe, vpe, 1e-5), waits=[t_d], inc="sv")
    t_sd = em.op("scalar", lambda a: a.activation(sd, vpe, AF.Sqrt), waits=[t_vpe], inc="sa")
    t_rstd = em.op("vector", lambda v: v.reciprocal(rstd, sd), waits=[t_sd], inc="sv")
    t_nm0 = em.op("vector", lambda v: v.tensor_mul(nmr, nmean, rstd), waits=[t_rstd], inc="sv")
    rstd2 = stats[:, 2, :]
    nmr2 = stats[:, 3, :]
    t_nmr = em.op("scalar", lambda a: a.copy(stats[:, 2:4, :],
                                             stats[:, 5:7, :]), waits=[t_nm0], inc="sa")
    t_xn = [em.op("vector", lambda v, tc=tc: v.tensor_scalar(x_tm[:, tc, :], x_tm[:, tc, :], rstd2[:, tc:tc + 1],
                                                             nmr2[:, tc:tc + 1], ALU.mult, ALU.add),
                  waits=tmax(t_nmr, t_s1), inc="sv") for tc in range(TC)]

    trcons = [None, None]
    t_colmm = [None] * 4
    t_xnbf = None
    t_rmm_last = None
    for grp in range(4):
        t_e1s = [None] * CH
        for cc in range(CH):
            bk = ps_tr[(grp * CH + cc) % 2]
            t_tr = None
            for j in range(4):
                tc = grp * 4 + j
                t_tr = em.op("tensor", lambda t, tc=tc, cc=cc, j=j, bk=bk: t.transpose(
                    bk[:, 128 * j:128 * j + 128], x_tm[:, tc, 128 * cc:128 * cc + 128], idf[:, :]),
                    waits=tmax(t_xn[tc], t_idf, trcons[(grp * CH + cc) % 2]), inc="sp" if j == 3 else None)
            t_e2 = em.op("scalar", lambda a, grp=grp, cc=cc, bk=bk: a.copy(
                xn_bf[:, cc, 512 * grp:512 * grp + 512], bk[:, :]), waits=[t_tr], inc="sa")
            t_xnbf = t_e2
            if debug == "p1a":
                trcons[(grp * CH + cc) % 2] = t_e2
                continue
            t_e1s[cc] = em.op("vector", lambda v, cc=cc, bk=bk: v.tensor_copy(
                colbuf[:, cc, :], bk[:, :]), waits=tmax(t_tr, t_rmm_last, t_e2), inc="sv")
            trcons[(grp * CH + cc) % 2] = tmax(t_e1s[cc], t_e2)
        if debug == "p1a" or _lvl < 2:
            continue
        for cc in range(CH):
            t_colmm[grp] = em.op("tensor", lambda t, grp=grp, cc=cc: t.matmul(
                ps_lg[grp][0:E, 0:512], wrsb[:, cc, :], colbuf[:, cc, :],
                start=(cc == 0), stop=(cc == CH - 1)), waits=tmax(t_e1s[cc], t_wr),
                inc="sp" if cc == CH - 1 else None)
        t_rmm_last = t_colmm[grp]

    if debug == "p1a":
        em.op("sync", lambda s: s.dma_start(dbg["xnbf"].rearrange("p (c t) -> p c t", c=CH), xn_bf[:, :, :]),
              waits=[t_xnbf], inc="so", amt=16)
        em.op("sync", lambda s: s.nop(), waits=[("so", 16)])
        p1p.close(); p1.close()
        finish(nc, em, root)
        return nc
    if _lvl >= 3:
        t_lg = [em.op("vector", lambda v, grp=grp: v.tensor_scalar_add(
            logits_sb[:, 512 * grp:512 * grp + 512], ps_lg[grp][0:E, 0:512], brc[:, :]),
            waits=tmax(t_colmm[grp], t_br), inc="sv") for grp in range(4)]
    else:
        t_lg = [t_xnbf] * 4
    if debug == "p1b":
        em.op("sync", lambda s: s.dma_start(dbg["xnbf"].rearrange("p (c t) -> p c t", c=CH), xn_bf[:, :, :]),
              waits=tmax([t_xnbf], t_lg[-1]), inc="so", amt=16)
        import os as _os
        if not _os.environ.get("SKIP_LG") and _lvl >= 3:
            em.op("sync", lambda s: s.dma_start(dbg["lg"], logits_sb[:, :]), waits=[t_lg[-1]], inc="so", amt=16)
            em.op("sync", lambda s: s.nop(), waits=[("so", 32)])
        else:
            em.op("sync", lambda s: s.nop(), waits=[("so", 16)])
        p1p.close(); p1.close()
        finish(nc, em, root)
        return nc
    t_lt = None
    for tc in range(TC):
        t_lt = em.op("tensor", lambda t, tc=tc: t.transpose(
            ps_lt[:, tc, :], logits_sb[:, 128 * tc:128 * tc + 128], idf[0:E, 0:E]),
            waits=tmax(t_lg[tc // 4], t_idf), inc="sp")
    mx, nmx, sexp, mp = (st2[:, i, :] for i in range(4))
    t_mx = em.op("vector", lambda v: v.reduce_max(mx.rearrange("p (c o) -> p c o", c=TC), ps_lt[:, :, :], axis=AX.X),
                 waits=[t_lt], inc="sv")
    t_nmx = em.op("vector", lambda v: v.tensor_scalar_mul(nmx, mx, -1.0), waits=[t_mx], inc="sv")
    t_ex = None
    for tc in range(TC):
        t_ex = em.op("scalar", lambda a, tc=tc: a.activation(exps[:, tc, :], ps_lt[:, tc, :], AF.Exp,
                                                             bias=nmx[:, tc:tc + 1],
                                                             accum_out=sexp[:, tc:tc + 1]),
                     waits=[t_nmx], inc="sa")
    t_mp = em.op("vector", lambda v: v.reciprocal(mp, sexp), waits=[t_ex], inc="sv")
    t_msk = None
    for tc in range(TC):
        t_msk = em.op("vector", lambda v, tc=tc: v.tensor_scalar(msk[:, tc, :], ps_lt[:, tc, :],
                                                                 mx[:, tc:tc + 1], None, ALU.is_equal),
                      waits=tmax(t_mx, t_mp), inc="sv")
    t_gtm = [em.op("scalar", lambda a, tc=tc: a.activation(gtm[:, tc, :], msk[:, tc, :], AF.Copy,
                                                           scale=mp[:, tc:tc + 1]),
                   waits=tmax(t_mp, t_msk), inc="sa") for tc in range(TC)]
    t_gfm = None
    gt_cons = [trcons[0], trcons[1]]
    for grp in range(4):
        bk = ps_gt[grp % 2]
        t_gt = None
        for j in range(4):
            tc = grp * 4 + j
            t_gt = em.op("tensor", lambda t, tc=tc, j=j, bk=bk: t.transpose(
                bk[0:E, 128 * j:128 * j + 128], gtm[:, tc, :], idf[:, :]),
                waits=tmax(t_gtm[tc], gt_cons[grp % 2]), inc="sp" if j == 3 else None)
        t_gfm = em.op("scalar", lambda a, grp=grp, bk=bk: a.copy(gates_fm[:, 512 * grp:512 * grp + 512],
                                                                 bk[0:E, :]), waits=[t_gt], inc="sa")
        gt_cons[grp % 2] = t_gfm

    if debug == "p1":
        em.op("sync", lambda s: s.dma_start(dbg["xnbf"].rearrange("p (c t) -> p c t", c=CH), xn_bf[:, :, :]),
              waits=[t_xnbf], inc="so", amt=16)
        em.op("sync", lambda s: s.dma_start(dbg["gates"], gates_fm[:, :]), waits=[t_gfm], inc="so", amt=16)
        em.op("sync", lambda s: s.nop(), waits=[("so", 32)])
        p1p.close(); p1.close()
        finish(nc, em, root)
        return nc

    t_p1_free = tmax(t_gfm, t_lt, t_msk, t_ex, *t_colmm, t_xnbf, t_sqa, t_gtm[-1])

    p1p.close()
    p1.close()

    p2r = contextlib.ExitStack()
    wqsb = sb(p2r, "wqsb", [128, CH, D], BF16, side="right")
    wksb = sb(p2r, "wksb", [128, CH, D], BF16, side="right")
    p2 = contextlib.ExitStack()
    wvsb = sb(p2, "wvsb", [128, CH, D], BF16)
    wosb = sb(p2, "wosb", [128, CH, D], BF16)
    qfm = sb(p2, "qfm", [128, H, CH, 32], BF16)
    kfm = sb(p2, "kfm", [128, H, CH, 32], BF16)
    vfm = sb(p2, "vfm", [128, H, CH, 32], BF16)
    qsw = sb(p2, "qsw", [128, H, CH, 32], BF16)
    attfm = sb(p2, "attfm", [128, CH, SEG], BF16)
    ethead = sb(p2, "ethead", [128, 2, 4, SEG], BF16)
    vperm = sb(p2, "vperm", [128, 2, 4, 72], BF16)
    kperm = sb(p2, "kperm", [128, 2, 4, DK], BF16)
    skperm = sb(p2, "skperm", [128, 2, 4, DK], BF16)
    sqh = sb(p2, "sqh", [128, 2, CH, 32], BF16)
    elut = sb(p2, "elut", [128, 2, 256], BF16)
    elue = sb(p2, "elue", [128, 2, 256], BF16)
    attpre = sb(p2, "attpre", [128, 2, 4, DK], BF16)
    tmpsb = sb(p2, "tmpsb", [128, 2, 4, DK], BF16)
    zbg = sb(p2, "zbg", [128, 2, DK], BF16)
    ghz = sb(p2, "ghz", [33, 2, DK], BF16)
    rz = sb(p2, "rz", [33, 2, DK], F32)
    rdc = sb(p2, "rdc", [128, 2, 4], F32)
    rmc = sb(p2, "rmc", [128, 2, 4], F32)

    p2p = contextlib.ExitStack()
    ps_s = [pt(p2p, "ps_s0", [128, 512]), pt(p2p, "ps_s64", [128, 512])]
    ps_t0 = pt(p2p, "ps_t0", [128, 1024], BF16)
    ps_t64 = pt(p2p, "ps_t64", [128, 1024], BF16)
    ps_a0 = pt(p2p, "ps_a0", [128, 7, 72])
    ps_a64 = pt(p2p, "ps_a64", [128, 2, 72])
    ps_mem = pt(p2p, "ps_mem", [128, 512])
    ps_zat = pt(p2p, "ps_zat", [128, 512])

    em.op("gpsimd", lambda g: g.dma_start(wqsb[:, :, :], wqt_h.rearrange("p (c d) -> p c d", c=CH)),
          inc="sgc", amt=16)
    em.op("gpsimd", lambda g: g.dma_start(wksb[:, :, :], wkt_h.rearrange("p (c d) -> p c d", c=CH)),
          inc="sgc", amt=16)
    em.op("gpsimd", lambda g: g.dma_start(wvsb[:, :, :], wvt_h.rearrange("p (c d) -> p c d", c=CH)),
          waits=[t_p1_free], inc="sgc", amt=16)
    em.op("gpsimd", lambda g: g.dma_start(wosb[:, :, :], wot_h.rearrange("p (c d) -> p c d", c=CH)),
          waits=[t_p1_free], inc="sgc", amt=16)
    t_attw = ("sgc", em.cnt["sgc"])
    t_wq = t_wk = t_wv = t_wo = t_attw

    t_vones = em.op("vector", lambda v: v.memset(vperm[:, :, :, 64:65], 1.0), inc="sv")
    cons = {"s0": None, "s64": None, "mem": None}
    slot_cons = {}
    t0_evac_rd = None
    t64_evac_rd = None
    t0_attfm_rd = None
    zat_rd = None
    a0_rd = None
    a64_rd = None

    t_qfm_readers = None
    t_memsb = t_mones
    t_mem_upd_last = None
    seg_tok = {}
    t_w1d = [None] * E
    t_w2d = [None] * E
    t_m1done = [None] * E
    t_m2done = [None] * E
    w1sb = None
    p3r = None
    t_qsw_free = None
    t_wout = [None] * CH
    t_att_ev_last = None
    pend_up = None

    def emit_up(par, th, t_comb):
        nonlocal t_att_ev_last, t0_attfm_rd
        t_aT = None
        for pi in range(2):
            t_aT = em.op("tensor", lambda t, pi=pi, par=par: t.transpose(
                ps_t0[:, 256 + 128 * pi:384 + 128 * pi], attpre[:, par, 2 * pi:2 * pi + 2, :], idb[:, :]),
                waits=tmax(t_comb, t_idb,
                           t0_evac_rd if pi == 0 else None,
                           t0_attfm_rd if pi == 0 else None),
                inc="sp" if pi == 1 else None)
        t_ev1 = em.op("vector", lambda v, th=th: v.tensor_copy(
            attfm[:, 0:4, th:th + 32], ps_t0[:, 256:384].rearrange("p (c j) -> p c j", c=4)),
            waits=tmax(t_aT, seg_tok.get("attfm_wout")), inc="sv")
        t_ev2 = em.op("vector", lambda v, th=th: v.tensor_copy(
            attfm[:, 4:CH, th:th + 32], ps_t0[:, 384:512].rearrange("p (c j) -> p c j", c=4)),
            waits=tmax(t_aT, seg_tok.get("attfm_wout")), inc="sv")
        t0_attfm_rd = tmax(t_ev1, t_ev2)
        slot_cons[("attpre", par)] = t_aT
        t_att_ev_last = tmax(t_ev1, t_ev2)

    for s in range(_ns):
        t_proj = {}
        for mi, (wsb, dst, t_wdma) in enumerate(((wqsb, qfm, t_wq), (wksb, kfm, t_wk), (wvsb, vfm, t_wv))):
            for oc in range(CH):
                gi = mi * CH + oc
                bname = "s0" if gi % 2 == 0 else "s64"
                bk = ps_s[gi % 2]
                t_mm = None
                for kk in range(CH):
                    t_mm = em.op("tensor", lambda t, wsb=wsb, oc=oc, kk=kk, s=s, bk=bk: t.matmul(
                        bk[:, :], wsb[:, kk, 128 * oc:128 * oc + 128], xn_bf[:, kk, SEG * s:SEG * s + SEG],
                        start=(kk == 0), stop=(kk == CH - 1)),
                        waits=tmax(t_wdma, t_xnbf, cons[bname],
                                   t_qfm_readers if (dst is qfm and oc == 0 and kk == 0) else None),
                        inc="sp" if kk == CH - 1 else None)
                t_ev = em.op("scalar", lambda a, dst=dst, oc=oc, bk=bk: a.copy(
                    dst[:, 0:H, oc, 0:32], bk[:, :].rearrange("p (h j) -> p h j", h=H)),
                    waits=[t_mm], inc="sa")
                cons[bname] = t_ev
                t_proj[(mi, oc)] = t_ev
        t_qev = t_proj[(0, CH - 1)]
        t_kev = t_proj[(1, CH - 1)]
        t_vev = t_proj[(2, CH - 1)]
        if _noqsw:
            t_qsw = [("sa", 0)]
        else:
            em.op("sync", lambda sy: sy.dma_start(qsw[0:64, :, :, :], qfm[64:128, :, :, :]),
                  waits=tmax(t_qev, t_qsw_free), inc="sq", amt=16)
            em.op("sync", lambda sy: sy.dma_start(qsw[64:128, :, :, :], qfm[0:64, :, :, :]),
                  waits=[t_qev], inc="sq", amt=16)
            t_qsw = [("sq", em.cnt["sq"])]

        if s == NSEG - 1:
            t_qkproj_done = t_proj[(1, CH - 1)]
            p2r.close()
            p3r = contextlib.ExitStack()
            w1sb = sb(p3r, "w1sb", [128, CH, HID], BF16, side="right")
            w1ap = w1_h[0].rearrange("(kk p) h -> p kk h", p=128)
            t_w1d[0] = em.op("gpsimd", lambda g, w1ap=w1ap: g.dma_start(w1sb[:, :, :], w1ap),
                             waits=[t_qkproj_done], inc="sg1", amt=16)

        for h in range(H):
            par = h % 2
            p = par
            th = 32 * h
            tok_tr = {}
            for (src, pref, fo) in ((kfm, "k", 0), (vfm, "v", 128)):
                t_src = t_kev if pref == "k" else t_vev
                for c in range(4):
                    b = 0 if c < 2 else 64
                    cc4 = (c % 2) * 4
                    bk = ps_t0 if b == 0 else ps_t64
                    off = fo + (c % 2) * 64
                    tok_tr[(pref, c)] = em.op("tensor", lambda t, src=src, b=b, cc4=cc4, h=h, off=off, bk=bk: t.transpose(
                        bk[:, off:off + 64], src[b:b + 64, h, cc4:cc4 + 4, 0:32], idb[b:b + 64, b:b + 64]),
                        waits=tmax(t_src, t_idb, t0_attfm_rd if b == 0 else t64_evac_rd),
                        inc="sp" if (pref == "v" and c % 2 == 1) else None)
            t_kpe = {}
            t_vpe = {}
            for b, bk, ci in ((0, ps_t0, 0), (64, ps_t64, 2)):
                t_kpe[b] = em.op("vector", lambda v, par=par, ci=ci, bk=bk: v.tensor_copy(
                    kperm[:, par, ci:ci + 2, :], bk[:, 0:128].rearrange("p (c k) -> p c k", c=2)),
                    waits=tmax(tok_tr[("v", ci + 1)], slot_cons.get(("kperm", par))), inc="sv")
                t_vpe[b] = em.op("vector", lambda v, par=par, ci=ci, bk=bk: v.tensor_copy(
                    vperm[:, par, ci:ci + 2, 0:64], bk[:, 128:256].rearrange("p (c k) -> p c k", c=2)),
                    waits=tmax(tok_tr[("v", ci + 1)], slot_cons.get(("vperm", par)), t_vones), inc="sv")
            t0_evac_rd = tmax(t_kpe[0], t_vpe[0])
            t64_evac_rd = tmax(t_kpe[64], t_vpe[64])
            t_kperm = tmax(t_kpe[0], t_kpe[64])
            t_vperm = tmax(t_vpe[0], t_vpe[64])

            t_e0 = em.op("scalar", lambda a, par=par: a.activation(
                elut[:, par, :], kperm[:, par, :, :], AF.Relu, scale=-1.0),
                waits=tmax(t_kperm, slot_cons.get(("elut", par))), inc="sa")
            t_e1 = em.op("scalar", lambda a, par=par: a.activation(elue[:, par, :], elut[:, par, :], AF.Exp,
                                                                   scale=-1.0),
                         waits=tmax(t_e0, slot_cons.get(("elue", par))), inc="sa")
            t_skp = em.op("vector", lambda v, par=par: v.scalar_tensor_tensor(
                skperm[:, par, :, :],
                kperm[:, par, :, :], 0.0,
                elue[:, par, :], ALU.max, ALU.add),
                waits=tmax(t_e1, slot_cons.get(("skperm", par))), inc="sv")
            t_q0 = em.op("scalar", lambda a, par=par, h=h: a.activation(
                elut[:, par, :], qfm[:, h, :, :], AF.Relu, scale=-1.0),
                waits=tmax(t_qev, t_skp), inc="sa")
            t_q1 = em.op("scalar", lambda a, par=par: a.activation(elue[:, par, :], elut[:, par, :], AF.Exp,
                                                                   scale=-1.0),
                         waits=[t_q0], inc="sa")
            t_sqh = em.op("vector", lambda v, par=par, h=h: v.scalar_tensor_tensor(
                sqh[:, par, :, :],
                qfm[:, h, :, :], 0.0,
                elue[:, par, :], ALU.max, ALU.add),
                waits=tmax(t_q1, slot_cons.get(("sqh", par))), inc="sv")
            slot_cons[("kperm", par)] = t_skp
            slot_cons[("elut", par)] = t_q1
            slot_cons[("elue", par)] = t_sqh
            t_qfm_readers = tmax(t_q0, t_sqh, t_qsw)

            t_S = {}

            def s_pair(tc_, h=h, par=par):
                b = 0 if tc_ < 2 else 64
                cc4 = (tc_ % 2) * 4
                bname = "s0" if b == 0 else "s64"
                bk = ps_s[0] if b == 0 else ps_s[1]
                rhs_even = qfm if b == 0 else (qfm if _noqsw else qsw)
                rhs_odd = (qfm if _noqsw else qsw) if b == 0 else qfm
                em.op("tensor", lambda t, b=b, cc4=cc4, h=h, bk=bk, re_=rhs_even: t.matmul(
                    bk[:, 0:256], kfm[b:b + 64, h, cc4:cc4 + 4, 0:32], re_[b:b + 64, h, 0:CH, 0:32],
                    start=True, stop=False, skip_group_check=True),
                    waits=tmax(t_kev, t_qev, t_qsw, cons[bname]))
                t_Smm = em.op("tensor", lambda t, b=b, cc4=cc4, h=h, bk=bk, ro=rhs_odd: t.matmul(
                    bk[:, 256:512], kfm[b:b + 64, h, cc4:cc4 + 4, 0:32], ro[b:b + 64, h, 0:CH, 0:32],
                    start=False, stop=True, skip_group_check=True), inc="sp")
                tE = em.op("scalar", lambda a, tc_=tc_, par=par, bk=bk: a.activation(
                    ethead[:, par, tc_, :], bk[:, :], AF.Exp, scale=0.125),
                    waits=tmax(t_Smm, slot_cons.get(("ethead", par))), inc="sa")
                t_S[tc_] = tE
                cons[bname] = tE
                return t_Smm

            s_pair(0)
            t_qsw_free = s_pair(2)

            if pend_up is not None:
                emit_up(*pend_up)
                pend_up = None

            s_pair(1)
            s_pair(3)

            zpart = 0 if h < 8 else 32
            zoff = 64 * (h % 8)
            t_z = None
            for tc_ in range(4):
                t_z = em.op("tensor", lambda t, tc_=tc_, par=par, zpart=zpart, zoff=zoff,
                            fz=(s == 0 and h in (0, 8) and tc_ == 0),
                            lz=(s == NSEG - 1 and h == H - 1 and tc_ == 3): t.matmul(
                    ps_zat[zpart:zpart + 1, zoff:zoff + 64], onesc[:, :], skperm[:, par, tc_, :],
                    start=fz, stop=lz, skip_group_check=True),
                    waits=tmax(t_skp, zat_rd if tc_ == 0 else None),
                    inc="sp" if tc_ == 3 else None)

            t_NU0 = t_NU = t_memmm = None
            if s > 0:
                for uc in (0, 1):
                    t_NU0 = em.op("tensor", lambda t, uc=uc, par=par, h=h: t.matmul(
                        ps_a0[:, 4 + uc, 0:65], sqh[0:64, par, 4 * uc:4 * uc + 4, :],
                        mem_sb[0:64, h, 0:65], start=True, stop=True, skip_group_check=True),
                        waits=tmax(t_sqh, t_memsb, a0_rd if uc == 0 else None), inc="sp")
                for uc in (2, 3):
                    t_NU = em.op("tensor", lambda t, uc=uc, par=par, h=h: t.matmul(
                        ps_a64[:, uc - 2, 0:65], sqh[64:128, par, 4 * (uc - 2):4 * (uc - 2) + 4, :],
                        mem_sb[64:128, h, 0:65], start=(uc == 2), stop=(uc == 3), skip_group_check=True),
                        waits=tmax(t_sqh, t_memsb, a64_rd if uc == 2 else None), inc="sp")
            for tc_ in (0, 1, 2, 3):
                t_memmm = em.op("tensor", lambda t, tc_=tc_, par=par, h=h,
                                fz=(s == 0 and h in (0, 1) and tc_ == 0),
                                lz=(s == NSEG - 1 and h == H - 1 and tc_ == 3): t.matmul(
                    _mem_ap(ps_mem, h), skperm[:, par, tc_, :], vperm[:, par, tc_, 0:64],
                    start=fz, stop=lz, skip_group_check=True),
                    waits=tmax(t_skp, t_vperm, cons["mem"] if tc_ == 0 else None),
                    inc="sp" if tc_ == 3 else None)
            t_mem_upd_last = t_memmm
            slot_cons[("skperm", par)] = t_memmm
            slot_cons[("sqh", par)] = t_NU if s > 0 else t_sqh

            t_zb = None
            if s > 0:
                t_rz = em.op("vector", lambda v, par=par, zpart=zpart, zoff=zoff: v.reciprocal(
                    rz[zpart:zpart + 1, par, :], ps_zat[zpart:zpart + 1, zoff:zoff + 64]),
                    waits=tmax(t_z, slot_cons.get(("rz", par))), inc="sv")
                zat_rd = t_rz
                t_gh0 = em.op("vector", lambda v, par=par, h=h, zpart=zpart: v.tensor_mul(
                    ghz[zpart:zpart + 1, par, :],
                    omgf[zpart:zpart + 1, 64 * h:64 * h + 64], rz[zpart:zpart + 1, par, :]),
                    waits=tmax(t_rz, t_omg, slot_cons.get(("ghz", par))), inc="sv")
                t_ghz = em.op("vector", lambda v, par=par, zpart=zpart: v.tensor_sub(
                    ghz[zpart:zpart + 1, par, :], rz[zpart:zpart + 1, par, :],
                    ghz[zpart:zpart + 1, par, :]),
                    waits=[t_gh0], inc="sv")
                t_zb = em.op("tensor", lambda t, par=par, zpart=zpart: t.matmul(
                    ps_a0[:, 6, 0:64], onesr[zpart:zpart + 1, :], ghz[zpart:zpart + 1, par, :],
                    start=True, stop=True,
                    skip_group_check=True), waits=tmax(t_ghz, a0_rd), inc="sp")
                slot_cons[("rz", par)] = t_ghz
                slot_cons[("ghz", par)] = t_zb

            t_AD = None
            first = True
            for tc_ in (0, 2, 1, 3):
                for uc in range(4):
                    t_AD = em.op("tensor", lambda t, tc_=tc_, uc=uc, par=par, st=first: t.matmul(
                        ps_a0[:, uc, 0:65], ethead[:, par, tc_, 128 * uc:128 * uc + 128], vperm[:, par, tc_, 0:65],
                        start=st, stop=(tc_ == 3 and uc == 3), skip_group_check=True),
                        waits=tmax(t_S[tc_], t_vperm, a0_rd if first else None),
                        inc="sp" if (tc_ == 3 and uc == 3) else None)
                    first = False
            slot_cons[("ethead", par)] = t_AD
            slot_cons[("vperm", par)] = tmax(t_memmm, t_AD)

            slotmap = {0: 0, 2: 1, 1: 2, 3: 3}
            t_rd = em.op("vector", lambda v, par=par: v.reciprocal(
                rdc[:, par, :], ps_a0[:, 0:4, 64:65]),
                waits=tmax(t_AD, slot_cons.get(("rd", par))), inc="sv")
            t_rm1 = t_rm = t_zbg = None
            if s > 0:
                t_rm1 = em.op("vector", lambda v, par=par: v.reciprocal(
                    rmc[:, par, 0:2], ps_a0[:, 4:6, 64:65]),
                    waits=tmax(slot_cons.get(("rm", par)), t_NU0), inc="sv")
                t_rm = em.op("vector", lambda v, par=par: v.reciprocal(
                    rmc[:, par, 2:4], ps_a64[:, 0:2, 64:65]), waits=[t_NU], inc="sv")
                t_zbg = em.op("vector", lambda v, par=par: v.tensor_copy(
                    zbg[:, par, :], ps_a0[:, 6, 0:64]),
                    waits=tmax(slot_cons.get(("zbg", par)), t_zb), inc="sv")
            t_st1s = []
            st2ls = []
            for uc in range(4):
                sl = slotmap[uc]
                t_st1 = em.op("vector", lambda v, uc=uc, sl=sl, par=par, h=h: v.scalar_tensor_tensor(
                    attpre[:, par, sl, :], ps_a0[:, uc, 0:64], rdc[:, par, uc:uc + 1],
                    omgf[:, 64 * h:64 * h + 64], ALU.mult, ALU.mult),
                    waits=tmax(t_rd, t_omg, slot_cons.get(("attpre", par))), inc="sv")
                t_st1s.append(t_st1)
                if s > 0:
                    nsrc = (ps_a0[:, 4 + uc, 0:64] if uc < 2 else ps_a64[:, uc - 2, 0:64])
                    t_st2l = em.op("vector", lambda v, uc=uc, sl=sl, par=par, nsrc=nsrc: v.scalar_tensor_tensor(
                        tmpsb[:, par, sl, :], nsrc, rmc[:, par, uc:uc + 1],
                        zbg[:, par, :], ALU.mult, ALU.mult),
                        waits=tmax(t_rm1, t_rm, t_zbg, slot_cons.get(("tmpsb", par))), inc="sv")
                    st2ls.append(t_st2l)
            if s > 0:
                t_comb = em.op("gpsimd", lambda g, par=par: g.tensor_add(
                    attpre[:, par, :, :], attpre[:, par, :, :], tmpsb[:, par, :, :]),
                    waits=tmax(*t_st1s, *st2ls), inc="sgc")
                slot_cons[("tmpsb", par)] = t_comb
            else:
                t_comb = tmax(*t_st1s)
            slot_cons[("rd", par)] = t_comb
            a0_rd = tmax(t_rd, *t_st1s)
            if s > 0:
                slot_cons[("rm", par)] = t_comb
                slot_cons[("zbg", par)] = t_comb
                a0_rd = tmax(a0_rd, t_rm1, t_zbg, st2ls[0], st2ls[1])
                a64_rd = tmax(t_rm, st2ls[2], st2ls[3])

            pend_up = (par, th, t_comb)
            if h == H - 1 or not DEFER_UP:
                emit_up(*pend_up)
                pend_up = None

        if s < NSEG - 1:
            t_me = None
            for h2 in range(H):
                t_me = em.op("vector", lambda v, h2=h2: v.tensor_copy(
                    mem_sb[0:64, h2, 0:64], _mem_ap(ps_mem, h2)), waits=[t_mem_upd_last], inc="sv")
                t_me = em.op("vector", lambda v, h2=h2: v.tensor_copy(
                    mem_sb[64:128, h2, 0:64], _mem_ap(ps_mem, h2)), inc="sv")
            t_memsb = t_me
            cons["mem"] = t_me

        for dd in range(CH):
            bname = "s0" if dd % 2 == 0 else "s64"
            bk = ps_s[dd % 2]
            t_mm = None
            for ccp in range(CH):
                t_mm = em.op("tensor", lambda t, dd=dd, ccp=ccp, bk=bk: t.matmul(
                    bk[:, :], wosb[:, ccp, 128 * dd:128 * dd + 128], attfm[:, ccp, :],
                    start=(ccp == 0), stop=(ccp == CH - 1)),
                    waits=tmax(t_wo, t_att_ev_last, cons[bname]), inc="sp" if ccp == CH - 1 else None)
            t_wout[dd] = em.op("scalar", lambda a, dd=dd, s=s, bk=bk: a.copy(
                out_acc[:, dd, SEG * s:SEG * s + SEG], bk[:, :]), waits=[t_mm], inc="sa")
            cons[bname] = t_wout[dd]
            if dd == CH - 1:
                seg_tok["attfm_wout"] = t_mm
    t_attn_out = tmax(*t_wout)

    if debug == "attn":
        em.op("sync", lambda s: s.dma_start(dbg["oacc"].rearrange("p (c t) -> p c t", c=CH)[:, :, 0:SEG * NSEG],
                                            out_acc[:, :, 0:SEG * NSEG]),
              waits=[t_attn_out], inc="so", amt=16)
        em.op("sync", lambda s: s.nop(), waits=[("so", 16)])
        p2p.close(); p2.close()
        if p3r is not None:
            p3r.close()
        finish(nc, em, root)
        return nc

    t_psum2_free = tmax(a0_rd, a64_rd, zat_rd,
                        t0_evac_rd, t64_evac_rd, t0_attfm_rd,
                        cons["mem"])
    t_p2_free = tmax(t_attn_out, t_att_ev_last, t_mem_upd_last, t_qfm_readers,
                     cons["s0"], cons["s64"], t_psum2_free)

    p2p.close()
    p2.close()

    p3 = contextlib.ExitStack()
    w2sb = sb(p3, "w2sb", [128, 16, D], BF16)
    b1sb = sb(p3, "b1sb", [128, 16, E], F32)
    b2bf = sb(p3, "b2bf", [E, D], BF16)
    ohbf = sb(p3, "ohbf", [E, E, 128], BF16)
    h1g = sb(p3, "h1g", [128, 2, 16, SEG], BF16)
    h1t = sb(p3, "h1t", [128, 2, SEG], BF16)
    gbc = sb(p3, "gbc", [128, 4, SEG], BF16)
    p3p = contextlib.ExitStack()
    ps_m1 = [pt(p3p, "ps_m1a", [128, 512]), pt(p3p, "ps_m1b", [128, 512])]
    ps_y = [pt(p3p, "ps_y0", [128, 512]), pt(p3p, "ps_y1", [128, 512])]
    ps_gb = pt(p3p, "ps_gb", [128, 512])

    em.op("gpsimd", lambda g: g.dma_start(b2bf[:, :], b2_h), waits=[t_p2_free], inc="sgc", amt=16)
    em.op("gpsimd", lambda g: g.dma_start(ohbf[:, :, :], oh_h.rearrange("e (f p) -> e f p", f=E)),
          waits=[t_p2_free], inc="sgc", amt=16)
    t_b2oh = ("sgc", em.cnt["sgc"])
    t_b2 = t_oh = t_b2oh
    t_w2d[0] = em.op("gpsimd", lambda g: g.dma_start(
        w2sb[:, :, :], w2_h[0].rearrange("(hh p) d -> p hh d", p=128)),
        waits=[t_p2_free], inc="sg2", amt=16)
    t_b1 = em.op("sync", lambda s: s.dma_start(b1sb[:, :, :], b1t_h.rearrange("p (c e) -> p c e", c=16)),
                 waits=[t_p2_free], inc="ss", amt=16)
    t_pb_free = tmax(cons["s0"], cons["s64"], t_psum2_free)
    fcons = {"m1a": t_pb_free, "m1b": t_pb_free, "y0": t_pb_free, "y1": t_pb_free, "gb": t_pb_free}
    h1g_cons = [None, None]
    h1t_cons = [None, None]
    gbc_cons = [None] * 4
    t_oacc = {(dd, tt): t_attn_out for dd in range(CH) for tt in range(4)}
    t_gate = {}
    t_m2pe = {}

    for e in range(E):
        t_gbc = [None] * 4
        for tt in range(4):
            t_g = em.op("tensor", lambda t, e=e, tt=tt: t.matmul(
                ps_gb[:, :], ohbf[0:E, e, :], gates_fm[0:E, SEG * tt:SEG * tt + SEG],
                start=True, stop=True), waits=tmax(t_gfm, t_oh, fcons["gb"]), inc="sp")
            t_gbc[tt] = em.op("scalar", lambda a, tt=tt: a.copy(gbc[:, tt, :], ps_gb[:, :]),
                              waits=tmax(t_g, gbc_cons[tt]), inc="sa")
            fcons["gb"] = t_gbc[tt]
            gbc_cons[tt] = None

        def m1(tt, e=e, t_gbc=t_gbc):
            slot = tt % 2
            t_last_mm = None
            for hh in range(16):
                bname = "m1a" if hh % 2 == 0 else "m1b"
                bk = ps_m1[hh % 2]
                t_mm = None
                for kk in range(CH):
                    t_mm = em.op("tensor", lambda t, hh=hh, kk=kk, tt=tt, bk=bk: t.matmul(
                        bk[:, :], w1sb[:, kk, 128 * hh:128 * hh + 128], xn_bf[:, kk, SEG * tt:SEG * tt + SEG],
                        start=(kk == 0), stop=(kk == CH - 1)),
                        waits=tmax(t_w1d[e], t_xnbf, fcons[bname],
                                   h1g_cons[slot] if kk == 0 and hh == 0 else None),
                        inc="sp" if kk == CH - 1 else None)
                t_last_mm = t_mm
                t_r = em.op("scalar", lambda a, hh=hh, e=e, slot=slot, bk=bk: a.activation(
                    h1t[:, slot, :], bk[:, :], AF.Relu, bias=b1sb[:, hh, e:e + 1]),
                    waits=tmax(t_mm, t_b1, h1t_cons[slot]), inc="sa")
                t_gt = em.op("vector", lambda v, hh=hh, slot=slot, tt=tt: v.tensor_mul(
                    h1g[:, slot, hh, :], h1t[:, slot, :], gbc[:, tt, :]),
                    waits=tmax(t_r, t_gbc[tt]), inc="sv")
                h1t_cons[slot] = t_gt
                fcons[bname] = t_r
                t_gate[(e, tt)] = t_gt
            return t_last_mm

        def m2(tt, e=e):
            slot = tt % 2
            t_last_mm = None
            for dd in range(CH):
                bname = "y0" if dd % 2 == 0 else "y1"
                bk = ps_y[dd % 2]
                t_mm = None
                if e == 0:
                    em.op("tensor", lambda t, dd=dd, tt=tt, bk=bk: t.matmul(
                        bk[:, :], b2bf[0:E, 128 * dd:128 * dd + 128], gates_fm[0:E, SEG * tt:SEG * tt + SEG],
                        start=True, stop=False, skip_group_check=True),
                        waits=tmax(t_b2, t_gfm, fcons[bname]))
                for hh in range(16):
                    t_mm = em.op("tensor", lambda t, dd=dd, hh=hh, slot=slot, e=e, bk=bk: t.matmul(
                        bk[:, :], w2sb[:, hh, 128 * dd:128 * dd + 128], h1g[:, slot, hh, :],
                        start=(e > 0 and hh == 0), stop=(hh == 15), skip_group_check=True),
                        waits=tmax(t_w2d[e], t_gate[(e, tt)], fcons[bname]),
                        inc="sp" if hh == 15 else None)
                t_last_mm = t_mm
                t_ev = em.op("vector", lambda v, dd=dd, tt=tt, bk=bk: v.tensor_tensor(
                    out_acc[:, dd, SEG * tt:SEG * tt + SEG], out_acc[:, dd, SEG * tt:SEG * tt + SEG],
                    bk[:, :], ALU.add), waits=tmax(t_mm, t_oacc[(dd, tt)]), inc="sv")
                t_oacc[(dd, tt)] = t_ev
                fcons[bname] = t_ev
            t_m2pe[(e, tt)] = t_last_mm
            return t_last_mm

        sched = [(m1, 0), (m1, 1), (m2, 0), (m1, 2), (m2, 1), (m1, 3), (m2, 2), (m2, 3)]
        for fn, tt in sched:
            r = fn(tt)
            if fn is m2:
                h1g_cons[tt % 2] = r
        t_m1done[e] = t_gate[(e, 3)]
        t_m2done[e] = tmax(t_m2pe[(e, 2)], t_m2pe[(e, 3)], t_oacc[(CH - 1, 3)])
        if e + 1 < E:
            w1ap = w1_h[e + 1].rearrange("(kk p) h -> p kk h", p=128)
            w2ap = w2_h[e + 1].rearrange("(hh p) d -> p hh d", p=128)
            t_w1d[e + 1] = em.op("gpsimd", lambda g, w1ap=w1ap: g.dma_start(w1sb[:, :, :], w1ap),
                                 waits=[t_m1done[e]], inc="sg1", amt=16)
            t_w2d[e + 1] = em.op("gpsimd", lambda g, w2ap=w2ap: g.dma_start(w2sb[:, :, :], w2ap),
                                 waits=[t_m2done[e]], inc="sg2", amt=16)

    t_ffn = tmax(*[t_oacc[(dd, tt)] for dd in range(CH) for tt in range(4)])

    p3p.close()
    p3.close()
    p3r.close()

    p4r = contextlib.ExitStack()
    x2a = sb(p4r, "x2a", [128, 8, D], F32, side="right")
    p4 = contextlib.ExitStack()
    x2b = sb(p4, "x2b", [128, 8, D], F32)
    ybuf = sb(p4, "ybuf", [128, 2, D], F32)
    p4p = contextlib.ExitStack()
    ps_o = [pt(p4p, "ps_o0", [128, 1024], BF16), pt(p4p, "ps_o1", [128, 1024], BF16)]

    t_x2a = em.op("sync", lambda s: s.dma_start(x2a[:, :, :], x_ap[:, 0:8, :]),
                  waits=[t_m1done[E - 1]], inc="sx", amt=16)
    t_x2b = em.op("sync", lambda s: s.dma_start(x2b[:, :, :], x_ap[:, 8:16, :]),
                  waits=[t_m2done[E - 1]], inc="sq", amt=16)
    ocons = [None, None]
    ydma = [None, None]
    n_out = 0
    y_ap = y_h.rearrange("(tc p) d -> tc p d", p=128)
    for tc in range(TC):
        x2 = x2a if tc < 8 else x2b
        t_x2 = t_x2a if tc < 8 else t_x2b
        xtc = tc if tc < 8 else tc - 8
        bk = ps_o[tc % 2]
        t_tr = None
        for dd in range(CH):
            t_tr = em.op("tensor", lambda t, tc=tc, dd=dd, bk=bk: t.transpose(
                bk[:, 128 * dd:128 * dd + 128], out_acc[:, dd, 128 * tc:128 * tc + 128], idb[:, :]),
                waits=tmax(t_ffn, t_idb, ocons[tc % 2]), inc="sp" if dd == CH - 1 else None)
        t_add = em.op("vector", lambda v, tc=tc, xtc=xtc, x2=x2, bk=bk: v.tensor_tensor(
            ybuf[:, tc % 2, :], x2[:, xtc, :], bk[:, :], ALU.add),
            waits=tmax(t_tr, t_x2, ydma[tc % 2]), inc="sv")
        ocons[tc % 2] = t_add
        ydma[tc % 2] = em.op("sync", lambda s, tc=tc: s.dma_start(y_ap[tc], ybuf[:, tc % 2, :]),
                             waits=[t_add], inc=("so" if tc % 2 == 0 else "so2"), amt=16)
        n_out += 16
    em.op("sync", lambda s: s.nop(), waits=[("so", n_out // 2), ("so2", n_out // 2)])

    p4p.close()
    p4.close()
    p4r.close()
    finish(nc, em, root)
    return nc


def finish(nc, em, root):
    semstack = contextlib.ExitStack()
    sems = {s: semstack.enter_context(nc.semaphore(s)) for s in ("sp", "sv", "sa", "ss", "sx", "sq", "sgc", "sg1", "sg2", "so", "so2")}
    with nc.Block() as block:
        @block.tensor
        def _(t):
            em.replay_engine("tensor", t, sems)

        @block.vector
        def _(v):
            em.replay_engine("vector", v, sems)

        @block.scalar
        def _(a):
            em.replay_engine("scalar", a, sems)

        @block.gpsimd
        def _(g):
            em.replay_engine("gpsimd", g, sems)

        @block.sync
        def _(s):
            em.replay_engine("sync", s, sems)
    semstack.close()
    root.close()


def _layouts(inputs):
    f = np.float32

    def wt(W):
        return np.ascontiguousarray(np.asarray(W, f).T.reshape(CH, 128, D).transpose(1, 0, 2)).reshape(128, CH * D)

    betas = np.asarray(inputs["L0_attn_betas"], f).reshape(1, H * DK)
    return {
        "wqt": wt(inputs["L0_attn_Wq"]), "wkt": wt(inputs["L0_attn_Wk"]),
        "wvt": wt(inputs["L0_attn_Wv"]), "wot": wt(inputs["L0_attn_Wout"]),
        "betab": np.ascontiguousarray(np.broadcast_to(betas, (128, H * DK))),
        "wr": np.ascontiguousarray(np.asarray(inputs["L0_ffn_Wr"], f).reshape(CH, 128, E).transpose(1, 0, 2)).reshape(128, CH * E),
        "br": np.ascontiguousarray(np.asarray(inputs["L0_ffn_br"], f).reshape(E, 1)),
        "w1": np.ascontiguousarray(np.asarray(inputs["L0_ffn_W1"], f)),
        "b1t": np.ascontiguousarray(np.asarray(inputs["L0_ffn_b1"], f).reshape(E, 16, 128).transpose(2, 1, 0)).reshape(128, 16 * E),
        "w2": np.ascontiguousarray(np.asarray(inputs["L0_ffn_W2"], f)),
        "b2": np.ascontiguousarray(np.asarray(inputs["L0_ffn_b2"], f)),
        "ident": np.eye(128, dtype=f),
        "oh": np.ascontiguousarray(np.repeat(np.eye(E, dtype=f)[:, :, None], 128, axis=2)).reshape(E, E * 128),
    }


def kernel(**inputs):
    x = np.ascontiguousarray(np.asarray(inputs["L0_attn_x"], np.float32))
    shared = _layouts(inputs)
    nc = build()
    in_maps = []
    for i in range(NCORE):
        m = dict(shared)
        m["x"] = np.ascontiguousarray(x[i])
        in_maps.append(m)
    res = run_bass_kernel_spmd(nc, in_maps, core_ids=list(range(NCORE)))
    return np.stack([res.results[i]["y"] for i in range(NCORE)], axis=0).astype(np.float32)


# version 10
# speedup vs baseline: 1.0598x; 1.0598x over previous
import contextlib
import numpy as np
import ml_dtypes
import concourse.bass as bass
import concourse.mybir as mybir
from concourse.bass_utils import run_bass_kernel_spmd

F32 = mybir.dt.float32
BF16 = mybir.dt.bfloat16
AF = mybir.ActivationFunctionType
ALU = mybir.AluOpType
AX = mybir.AxisListType

D = 1024
T = 2048
SEG = 512
NSEG = 4
H = 16
DK = 64
E = 8
HID = 2048
CH = 8
TC = 16
NCORE = 8
DEFER_UP = True

ENGINES = ("tensor", "vector", "scalar", "gpsimd", "sync")


class Em:

    def __init__(self):
        self.q = {e: [] for e in ENGINES}
        self.cnt = {s: 0 for s in ("sp", "sv", "sa", "ss", "sx", "sq", "sgc", "sg1", "sg2", "so", "so2")}

    def op(self, eng, fn, waits=(), inc=None, amt=1):
        flat = []
        for w in waits:
            if w is None:
                continue
            if isinstance(w, list):
                flat.extend(w)
            else:
                flat.append(w)
        ws = tuple(flat)
        tok = None
        if inc is not None:
            self.cnt[inc] += amt
            tok = (inc, self.cnt[inc])
        self.q[eng].append((fn, ws, inc, amt))
        return tok

    def replay_engine(self, eng, e, sems):
        last = {}
        for fn, ws, inc, amt in self.q[eng]:
            for s, v in ws:
                if last.get(s, 0) >= v:
                    continue
                last[s] = v
                e.wait_ge(sems[s], v)
            ins = fn(e)
            if inc is not None:
                ins.then_inc(sems[inc], amt)


def tmax(*toks):
    best = {}
    for t in toks:
        if t is None:
            continue
        items = t if isinstance(t, list) else [t]
        for s, v in items:
            if best.get(s, 0) < v:
                best[s] = v
    out = [(s, v) for s, v in best.items()]
    return out


def _mem_ap(ps_mem, h):
    p0 = 64 * (h % 2)
    f0 = 64 * (h // 2)
    return ps_mem[p0:p0 + 64, f0:f0 + 64]


def build(debug=None):
    import os as _os
    _lvl = int(_os.environ.get("P1B_LVL", "9"))
    _ns = int(_os.environ.get("ATTN_NS", str(NSEG)))
    _noqsw = bool(_os.environ.get("NOQSW"))
    nc = bass.Bass("TRN2", target_bir_lowering=False)

    x_h = nc.dram_tensor("x", [T, D], F32, kind="ExternalInput").ap()
    wqt_h = nc.dram_tensor("wqt", [128, CH * D], F32, kind="ExternalInput").ap()
    wkt_h = nc.dram_tensor("wkt", [128, CH * D], F32, kind="ExternalInput").ap()
    wvt_h = nc.dram_tensor("wvt", [128, CH * D], F32, kind="ExternalInput").ap()
    wot_h = nc.dram_tensor("wot", [128, CH * D], F32, kind="ExternalInput").ap()
    betab_h = nc.dram_tensor("betab", [128, H * DK], F32, kind="ExternalInput").ap()
    wr_h = nc.dram_tensor("wr", [128, CH * E], F32, kind="ExternalInput").ap()
    br_h = nc.dram_tensor("br", [E, 1], F32, kind="ExternalInput").ap()
    w1_h = nc.dram_tensor("w1", [E, D, HID], F32, kind="ExternalInput").ap()
    b1t_h = nc.dram_tensor("b1t", [128, 16 * E], F32, kind="ExternalInput").ap()
    w2_h = nc.dram_tensor("w2", [E, HID, D], F32, kind="ExternalInput").ap()
    b2_h = nc.dram_tensor("b2", [E, D], F32, kind="ExternalInput").ap()
    ident_h = nc.dram_tensor("ident", [128, 128], F32, kind="ExternalInput").ap()
    oh_h = nc.dram_tensor("oh", [E, E * 128], F32, kind="ExternalInput").ap()
    y_h = nc.dram_tensor("y", [T, D], F32, kind="ExternalOutput").ap()

    dbg = {}
    if debug in ("p1", "p1a", "p1b"):
        dbg["xnbf"] = nc.dram_tensor("d_xnbf", [128, CH * T], BF16, kind="ExternalOutput").ap()
        if debug == "p1":
            dbg["gates"] = nc.dram_tensor("d_gates", [E, T], BF16, kind="ExternalOutput").ap()
        if debug == "p1b":
            dbg["lg"] = nc.dram_tensor("d_lg", [E, T], F32, kind="ExternalOutput").ap()
    if debug == "attn":
        dbg["oacc"] = nc.dram_tensor("d_oacc", [128, CH * T], BF16, kind="ExternalOutput").ap()

    em = Em()
    root = contextlib.ExitStack()

    def sb(st, name, shape, dtype, side=None):
        return st.enter_context(nc.sbuf_tensor(name, shape, dtype, side=side))

    def pt(st, name, shape, dtype=F32):
        return st.enter_context(nc.psum_tensor(name, shape, dtype))

    xn_bf = sb(root, "xn_bf", [128, CH, T], BF16)
    out_acc = sb(root, "out_acc", [128, CH, T], BF16)
    gates_fm = sb(root, "gates_fm", [E, T], BF16)
    mem_sb = sb(root, "mem_sb", [128, H, 72], BF16)
    idf = sb(root, "idf", [128, 128], F32)
    idb = sb(root, "idb", [128, 128], BF16)
    omgf = sb(root, "omgf", [128, H * DK], BF16)
    onesr = sb(root, "onesr", [33, 128], BF16)
    onesc = sb(root, "onesc", [128, 1], BF16)

    em.op("sync", lambda s: s.dma_start(idf[:, :], ident_h), inc="ss", amt=16)
    t_idb = em.op("gpsimd", lambda g: g.dma_start(idb[:, :], ident_h), inc="sgc", amt=16)
    em.op("vector", lambda v: v.memset(onesr[:, :], 1.0))
    em.op("vector", lambda v: v.memset(onesc[:, :], 1.0))
    t_mones = em.op("vector", lambda v: v.memset(mem_sb[:, :, 64:65], 1.0), inc="sv")

    p1 = contextlib.ExitStack()
    x_tm = sb(p1, "x_tm", [128, TC, D], F32)
    wrsb = sb(p1, "wrsb", [128, CH, E], F32)
    brc = sb(p1, "brc", [E, 1], F32)
    betf = sb(p1, "betf", [128, H * DK], F32)
    stats = sb(p1, "stats", [128, 8, TC], F32)
    st2 = sb(p1, "st2", [128, 4, TC], F32)
    colbuf = sb(p1, "colbuf", [128, CH, 512], F32)
    logits_sb = sb(p1, "logits_sb", [E, T], F32)
    sqscr = sb(p1, "sqscr", [128, 1, D], F32)
    msk = sb(p1, "msk", [128, TC, E], F32)
    exps = sb(p1, "exps", [128, TC, E], F32)
    gtm = sb(p1, "gtm", [128, TC, E], F32)

    p1p = contextlib.ExitStack()
    ps_tr = [pt(p1p, "ps_tr0", [128, 512]), pt(p1p, "ps_tr1", [128, 512])]
    ps_lg = [pt(p1p, f"ps_lg{i}", [E, 512]) for i in range(4)]
    ps_lt = pt(p1p, "ps_lt", [128, TC, E])
    ps_gt = ps_tr

    em.op("sync", lambda s: s.dma_start(wrsb[:, :, :], wr_h.rearrange("p (c e) -> p c e", c=CH)), inc="ss", amt=16)
    em.op("sync", lambda s: s.dma_start(brc[:, :], br_h), inc="ss", amt=16)
    em.op("sync", lambda s: s.dma_start(betf[:, :], betab_h), inc="ss", amt=16)
    t_consts = ("ss", em.cnt["ss"])
    t_wr = t_br = t_bet = t_idf = t_consts
    t_sig = em.op("scalar", lambda a: a.activation(betf[:, :], betf[:, :], AF.Sigmoid), waits=[t_consts], inc="sa")
    t_omg = em.op("vector", lambda v: v.tensor_scalar(omgf[:, :], betf[:, :], -1.0, 1.0, ALU.mult, ALU.add),
                  waits=[t_sig], inc="sv")
    x_ap = x_h.rearrange("(tc p) d -> p tc d", p=128)
    for q in range(4):
        em.op("sync", lambda s, q=q: s.dma_start(x_tm[:, 4 * q:4 * q + 4, :], x_ap[:, 4 * q:4 * q + 4, :]),
              inc="sx", amt=16)
    t_xall = ("sx", em.cnt["sx"])
    t_x = [t_xall] * 4

    t_s1 = t_sqa = None
    sq_tok = [None, None]
    for tc in range(TC):
        t_s1 = em.op("vector", lambda v, tc=tc: v.reduce_sum(stats[:, 0, tc:tc + 1], x_tm[:, tc, :], axis=AX.X),
                     waits=[t_x[tc // 4]], inc="sv")
        t_sqa = em.op("scalar", lambda a, tc=tc: a.activation(sqscr[:, 0, :], x_tm[:, tc, :], AF.Square,
                                                              accum_out=stats[:, 1, tc:tc + 1]),
                      waits=tmax(t_x[tc // 4], sq_tok[0]), inc="sa")
        sq_tok[0] = t_sqa
    nmean, gm2, vpe, rstd, nmr, sd = (stats[:, i, :] for i in range(2, 8))
    t_a = em.op("vector", lambda v: v.tensor_scalar_mul(nmean, stats[:, 0, :], -1.0 / D), waits=[t_s1], inc="sv")
    t_b = em.op("vector", lambda v: v.tensor_mul(gm2, nmean, nmean), waits=[t_a], inc="sv")
    t_c = em.op("vector", lambda v: v.tensor_scalar_mul(vpe, stats[:, 1, :], 1.0 / D), waits=[t_sqa], inc="sv")
    t_d = em.op("vector", lambda v: v.tensor_sub(vpe, vpe, gm2), waits=tmax(t_b, t_c), inc="sv")
    t_vpe = em.op("vector", lambda v: v.tensor_scalar_add(vpe, vpe, 1e-5), waits=[t_d], inc="sv")
    t_sd = em.op("scalar", lambda a: a.activation(sd, vpe, AF.Sqrt), waits=[t_vpe], inc="sa")
    t_rstd = em.op("vector", lambda v: v.reciprocal(rstd, sd), waits=[t_sd], inc="sv")
    t_nm0 = em.op("vector", lambda v: v.tensor_mul(nmr, nmean, rstd), waits=[t_rstd], inc="sv")
    rstd2 = stats[:, 2, :]
    nmr2 = stats[:, 3, :]
    t_nmr = em.op("scalar", lambda a: a.copy(stats[:, 2:4, :],
                                             stats[:, 5:7, :]), waits=[t_nm0], inc="sa")
    t_xn = [em.op("vector", lambda v, tc=tc: v.tensor_scalar(x_tm[:, tc, :], x_tm[:, tc, :], rstd2[:, tc:tc + 1],
                                                             nmr2[:, tc:tc + 1], ALU.mult, ALU.add),
                  waits=tmax(t_nmr, t_s1), inc="sv") for tc in range(TC)]

    trcons = [None, None]
    t_colmm = [None] * 4
    t_xnbf = None
    t_rmm_last = None
    for grp in range(4):
        t_e1s = [None] * CH
        for cc in range(CH):
            bk = ps_tr[(grp * CH + cc) % 2]
            t_tr = None
            for j in range(4):
                tc = grp * 4 + j
                t_tr = em.op("tensor", lambda t, tc=tc, cc=cc, j=j, bk=bk: t.transpose(
                    bk[:, 128 * j:128 * j + 128], x_tm[:, tc, 128 * cc:128 * cc + 128], idf[:, :]),
                    waits=tmax(t_xn[tc], t_idf, trcons[(grp * CH + cc) % 2]), inc="sp" if j == 3 else None)
            t_e2 = em.op("scalar", lambda a, grp=grp, cc=cc, bk=bk: a.copy(
                xn_bf[:, cc, 512 * grp:512 * grp + 512], bk[:, :]), waits=[t_tr], inc="sa")
            t_xnbf = t_e2
            if debug == "p1a":
                trcons[(grp * CH + cc) % 2] = t_e2
                continue
            t_e1s[cc] = em.op("vector", lambda v, cc=cc, bk=bk: v.tensor_copy(
                colbuf[:, cc, :], bk[:, :]), waits=tmax(t_tr, t_rmm_last, t_e2), inc="sv")
            trcons[(grp * CH + cc) % 2] = tmax(t_e1s[cc], t_e2)
        if debug == "p1a" or _lvl < 2:
            continue
        for cc in range(CH):
            t_colmm[grp] = em.op("tensor", lambda t, grp=grp, cc=cc: t.matmul(
                ps_lg[grp][0:E, 0:512], wrsb[:, cc, :], colbuf[:, cc, :],
                start=(cc == 0), stop=(cc == CH - 1)), waits=tmax(t_e1s[cc], t_wr),
                inc="sp" if cc == CH - 1 else None)
        t_rmm_last = t_colmm[grp]

    if debug == "p1a":
        em.op("sync", lambda s: s.dma_start(dbg["xnbf"].rearrange("p (c t) -> p c t", c=CH), xn_bf[:, :, :]),
              waits=[t_xnbf], inc="so", amt=16)
        em.op("sync", lambda s: s.nop(), waits=[("so", 16)])
        p1p.close(); p1.close()
        finish(nc, em, root)
        return nc
    if _lvl >= 3:
        t_lg = [em.op("vector", lambda v, grp=grp: v.tensor_scalar_add(
            logits_sb[:, 512 * grp:512 * grp + 512], ps_lg[grp][0:E, 0:512], brc[:, :]),
            waits=tmax(t_colmm[grp], t_br), inc="sv") for grp in range(4)]
    else:
        t_lg = [t_xnbf] * 4
    if debug == "p1b":
        em.op("sync", lambda s: s.dma_start(dbg["xnbf"].rearrange("p (c t) -> p c t", c=CH), xn_bf[:, :, :]),
              waits=tmax([t_xnbf], t_lg[-1]), inc="so", amt=16)
        import os as _os
        if not _os.environ.get("SKIP_LG") and _lvl >= 3:
            em.op("sync", lambda s: s.dma_start(dbg["lg"], logits_sb[:, :]), waits=[t_lg[-1]], inc="so", amt=16)
            em.op("sync", lambda s: s.nop(), waits=[("so", 32)])
        else:
            em.op("sync", lambda s: s.nop(), waits=[("so", 16)])
        p1p.close(); p1.close()
        finish(nc, em, root)
        return nc
    t_lt = None
    for tc in range(TC):
        t_lt = em.op("tensor", lambda t, tc=tc: t.transpose(
            ps_lt[:, tc, :], logits_sb[:, 128 * tc:128 * tc + 128], idf[0:E, 0:E]),
            waits=tmax(t_lg[tc // 4], t_idf), inc="sp")
    mx, nmx, sexp, mp = (st2[:, i, :] for i in range(4))
    t_mx = em.op("vector", lambda v: v.reduce_max(mx.rearrange("p (c o) -> p c o", c=TC), ps_lt[:, :, :], axis=AX.X),
                 waits=[t_lt], inc="sv")
    t_nmx = em.op("vector", lambda v: v.tensor_scalar_mul(nmx, mx, -1.0), waits=[t_mx], inc="sv")
    t_ex = None
    for tc in range(TC):
        t_ex = em.op("scalar", lambda a, tc=tc: a.activation(exps[:, tc, :], ps_lt[:, tc, :], AF.Exp,
                                                             bias=nmx[:, tc:tc + 1],
                                                             accum_out=sexp[:, tc:tc + 1]),
                     waits=[t_nmx], inc="sa")
    t_mp = em.op("vector", lambda v: v.reciprocal(mp, sexp), waits=[t_ex], inc="sv")
    t_msk = None
    for tc in range(TC):
        t_msk = em.op("vector", lambda v, tc=tc: v.tensor_scalar(msk[:, tc, :], ps_lt[:, tc, :],
                                                                 mx[:, tc:tc + 1], None, ALU.is_equal),
                      waits=tmax(t_mx, t_mp), inc="sv")
    t_gtm = [em.op("scalar", lambda a, tc=tc: a.activation(gtm[:, tc, :], msk[:, tc, :], AF.Copy,
                                                           scale=mp[:, tc:tc + 1]),
                   waits=tmax(t_mp, t_msk), inc="sa") for tc in range(TC)]
    t_gfm = None
    gt_cons = [trcons[0], trcons[1]]
    for grp in range(4):
        bk = ps_gt[grp % 2]
        t_gt = None
        for j in range(4):
            tc = grp * 4 + j
            t_gt = em.op("tensor", lambda t, tc=tc, j=j, bk=bk: t.transpose(
                bk[0:E, 128 * j:128 * j + 128], gtm[:, tc, :], idf[:, :]),
                waits=tmax(t_gtm[tc], gt_cons[grp % 2]), inc="sp" if j == 3 else None)
        t_gfm = em.op("scalar", lambda a, grp=grp, bk=bk: a.copy(gates_fm[:, 512 * grp:512 * grp + 512],
                                                                 bk[0:E, :]), waits=[t_gt], inc="sa")
        gt_cons[grp % 2] = t_gfm

    if debug == "p1":
        em.op("sync", lambda s: s.dma_start(dbg["xnbf"].rearrange("p (c t) -> p c t", c=CH), xn_bf[:, :, :]),
              waits=[t_xnbf], inc="so", amt=16)
        em.op("sync", lambda s: s.dma_start(dbg["gates"], gates_fm[:, :]), waits=[t_gfm], inc="so", amt=16)
        em.op("sync", lambda s: s.nop(), waits=[("so", 32)])
        p1p.close(); p1.close()
        finish(nc, em, root)
        return nc

    t_p1_free = tmax(t_gfm, t_lt, t_msk, t_ex, *t_colmm, t_xnbf, t_sqa, t_gtm[-1])

    p1p.close()
    p1.close()

    p2r = contextlib.ExitStack()
    wqsb = sb(p2r, "wqsb", [128, CH, D], BF16, side="right")
    wksb = sb(p2r, "wksb", [128, CH, D], BF16, side="right")
    p2 = contextlib.ExitStack()
    wvsb = sb(p2, "wvsb", [128, CH, D], BF16)
    wosb = sb(p2, "wosb", [128, CH, D], BF16)
    qfm = sb(p2, "qfm", [128, H, CH, 32], BF16)
    kfm = sb(p2, "kfm", [128, H, CH, 32], BF16)
    vfm = sb(p2, "vfm", [128, H, CH, 32], BF16)
    qsw = sb(p2, "qsw", [128, H, CH, 32], BF16)
    attfm = sb(p2, "attfm", [128, CH, SEG], BF16)
    ethead = sb(p2, "ethead", [128, 2, 4, SEG], BF16)
    vperm = sb(p2, "vperm", [128, 2, 4, 72], BF16)
    kperm = sb(p2, "kperm", [128, 2, 4, DK], BF16)
    skperm = sb(p2, "skperm", [128, 2, 4, DK], BF16)
    sqh = sb(p2, "sqh", [128, 2, CH, 32], BF16)
    elut = sb(p2, "elut", [128, 2, 256], BF16)
    elue = sb(p2, "elue", [128, 2, 256], BF16)
    attpre = sb(p2, "attpre", [128, 2, 4, DK], BF16)
    tmpsb = sb(p2, "tmpsb", [128, 2, 4, DK], BF16)
    zbg = sb(p2, "zbg", [128, 2, DK], BF16)
    ghz = sb(p2, "ghz", [33, 2, DK], BF16)
    rz = sb(p2, "rz", [33, 2, DK], F32)
    rr = sb(p2, "rr", [128, 2, 8], F32)

    p2p = contextlib.ExitStack()
    ps_s = [pt(p2p, "ps_s0", [128, 512]), pt(p2p, "ps_s64", [128, 512])]
    ps_t0 = pt(p2p, "ps_t0", [128, 1024], BF16)
    ps_t64 = pt(p2p, "ps_t64", [128, 1024], BF16)
    ps_a0 = pt(p2p, "ps_a0", [128, 7, 72])
    ps_a64 = pt(p2p, "ps_a64", [128, 2, 72])
    ps_mem = pt(p2p, "ps_mem", [128, 512])
    ps_zat = pt(p2p, "ps_zat", [128, 512])

    em.op("gpsimd", lambda g: g.dma_start(wqsb[:, :, :], wqt_h.rearrange("p (c d) -> p c d", c=CH)),
          inc="sgc", amt=16)
    em.op("gpsimd", lambda g: g.dma_start(wksb[:, :, :], wkt_h.rearrange("p (c d) -> p c d", c=CH)),
          inc="sgc", amt=16)
    em.op("gpsimd", lambda g: g.dma_start(wvsb[:, :, :], wvt_h.rearrange("p (c d) -> p c d", c=CH)),
          waits=[t_p1_free], inc="sgc", amt=16)
    em.op("gpsimd", lambda g: g.dma_start(wosb[:, :, :], wot_h.rearrange("p (c d) -> p c d", c=CH)),
          waits=[t_p1_free], inc="sgc", amt=16)
    t_attw = ("sgc", em.cnt["sgc"])
    t_wq = t_wk = t_wv = t_wo = t_attw

    t_vones = em.op("vector", lambda v: v.memset(vperm[:, :, :, 64:65], 1.0), inc="sv")
    cons = {"s0": None, "s64": None, "mem": None}
    slot_cons = {}
    t0_evac_rd = None
    t64_evac_rd = None
    t0_attfm_rd = None
    zat_rd = None
    a0_rd = None
    a64_rd = None

    t_qfm_readers = None
    t_memsb = t_mones
    t_mem_upd_last = None
    seg_tok = {}
    t_w1d = [None] * E
    t_w2d = [None] * E
    t_m1done = [None] * E
    t_m2done = [None] * E
    w1sb = None
    p3r = None
    t_qsw_free = None
    t_wout = [None] * CH
    t_att_ev_last = None
    pend_up = None
    pend_comb = None

    def emit_combine(par, h, sg, t_AD, t_NU0, t_NU, t_zb):
        nonlocal a0_rd, a64_rd, pend_up
        slotmap = {0: 0, 2: 1, 1: 2, 3: 3}
        if sg:
            t_rd = em.op("vector", lambda v, par=par: v.reciprocal(
                rr[:, par, 0:6], ps_a0[:, 0:6, 64:65]),
                waits=tmax(t_AD, t_NU0, slot_cons.get(("rd", par))), inc="sv")
            t_rm = em.op("vector", lambda v, par=par: v.reciprocal(
                rr[:, par, 6:8], ps_a64[:, 0:2, 64:65]), waits=[t_NU], inc="sv")
            t_zbg = em.op("vector", lambda v, par=par: v.tensor_copy(
                zbg[:, par, :], ps_a0[:, 6, 0:64]),
                waits=tmax(slot_cons.get(("zbg", par)), t_zb), inc="sv")
        else:
            t_rd = em.op("vector", lambda v, par=par: v.reciprocal(
                rr[:, par, 0:4], ps_a0[:, 0:4, 64:65]),
                waits=tmax(t_AD, slot_cons.get(("rd", par))), inc="sv")
            t_rm = t_zbg = None
        t_st1s = []
        st2ls = []
        for uc in range(4):
            sl = slotmap[uc]
            t_st1 = em.op("vector", lambda v, uc=uc, sl=sl, par=par, h=h: v.scalar_tensor_tensor(
                attpre[:, par, sl, :], ps_a0[:, uc, 0:64], rr[:, par, uc:uc + 1],
                omgf[:, 64 * h:64 * h + 64], ALU.mult, ALU.mult),
                waits=tmax(t_rd, t_omg, slot_cons.get(("attpre", par))), inc="sv")
            t_st1s.append(t_st1)
            if sg:
                nsrc = (ps_a0[:, 4 + uc, 0:64] if uc < 2 else ps_a64[:, uc - 2, 0:64])
                t_st2l = em.op("vector", lambda v, uc=uc, sl=sl, par=par, nsrc=nsrc: v.scalar_tensor_tensor(
                    tmpsb[:, par, sl, :], nsrc, rr[:, par, 4 + uc:5 + uc],
                    zbg[:, par, :], ALU.mult, ALU.mult),
                    waits=tmax(t_rm, t_zbg, slot_cons.get(("tmpsb", par))), inc="sv")
                st2ls.append(t_st2l)
        if sg:
            t_comb = em.op("gpsimd", lambda g, par=par: g.tensor_add(
                attpre[:, par, :, :], attpre[:, par, :, :], tmpsb[:, par, :, :]),
                waits=tmax(*t_st1s, *st2ls), inc="sgc")
            slot_cons[("tmpsb", par)] = t_comb
        else:
            t_comb = tmax(*t_st1s)
        slot_cons[("rd", par)] = t_comb
        a0_rd = tmax(t_rd, *t_st1s)
        if sg:
            slot_cons[("zbg", par)] = t_comb
            a0_rd = tmax(a0_rd, t_zbg, st2ls[0], st2ls[1])
            a64_rd = tmax(t_rm, st2ls[2], st2ls[3])
        pend_up = (par, 32 * h, t_comb)

    def emit_up(par, th, t_comb):
        nonlocal t_att_ev_last, t0_attfm_rd
        t_aT = None
        for pi in range(2):
            t_aT = em.op("tensor", lambda t, pi=pi, par=par: t.transpose(
                ps_t0[:, 256 + 128 * pi:384 + 128 * pi], attpre[:, par, 2 * pi:2 * pi + 2, :], idb[:, :]),
                waits=tmax(t_comb, t_idb,
                           t0_evac_rd if pi == 0 else None,
                           t0_attfm_rd if pi == 0 else None),
                inc="sp" if pi == 1 else None)
        t_ev1 = em.op("vector", lambda v, th=th: v.tensor_copy(
            attfm[:, 0:4, th:th + 32], ps_t0[:, 256:384].rearrange("p (c j) -> p c j", c=4)),
            waits=tmax(t_aT, seg_tok.get("attfm_wout")), inc="sv")
        t_ev2 = em.op("vector", lambda v, th=th: v.tensor_copy(
            attfm[:, 4:CH, th:th + 32], ps_t0[:, 384:512].rearrange("p (c j) -> p c j", c=4)),
            waits=tmax(t_aT, seg_tok.get("attfm_wout")), inc="sv")
        t0_attfm_rd = tmax(t_ev1, t_ev2)
        slot_cons[("attpre", par)] = t_aT
        t_att_ev_last = tmax(t_ev1, t_ev2)

    for s in range(_ns):
        t_proj = {}
        for mi, (wsb, dst, t_wdma) in enumerate(((wqsb, qfm, t_wq), (wksb, kfm, t_wk), (wvsb, vfm, t_wv))):
            for oc in range(CH):
                gi = mi * CH + oc
                bname = "s0" if gi % 2 == 0 else "s64"
                bk = ps_s[gi % 2]
                t_mm = None
                for kk in range(CH):
                    t_mm = em.op("tensor", lambda t, wsb=wsb, oc=oc, kk=kk, s=s, bk=bk: t.matmul(
                        bk[:, :], wsb[:, kk, 128 * oc:128 * oc + 128], xn_bf[:, kk, SEG * s:SEG * s + SEG],
                        start=(kk == 0), stop=(kk == CH - 1)),
                        waits=tmax(t_wdma, t_xnbf, cons[bname],
                                   t_qfm_readers if (dst is qfm and oc == 0 and kk == 0) else None),
                        inc="sp" if kk == CH - 1 else None)
                t_ev = em.op("scalar", lambda a, dst=dst, oc=oc, bk=bk: a.copy(
                    dst[:, 0:H, oc, 0:32], bk[:, :].rearrange("p (h j) -> p h j", h=H)),
                    waits=[t_mm], inc="sa")
                cons[bname] = t_ev
                t_proj[(mi, oc)] = t_ev
        t_qev = t_proj[(0, CH - 1)]
        t_kev = t_proj[(1, CH - 1)]
        t_vev = t_proj[(2, CH - 1)]
        if _noqsw:
            t_qsw = [("sa", 0)]
        else:
            em.op("sync", lambda sy: sy.dma_start(qsw[0:64, :, :, :], qfm[64:128, :, :, :]),
                  waits=tmax(t_qev, t_qsw_free), inc="sq", amt=16)
            em.op("sync", lambda sy: sy.dma_start(qsw[64:128, :, :, :], qfm[0:64, :, :, :]),
                  waits=[t_qev], inc="sq", amt=16)
            t_qsw = [("sq", em.cnt["sq"])]

        if s == NSEG - 1:
            t_qkproj_done = t_proj[(1, CH - 1)]
            p2r.close()
            p3r = contextlib.ExitStack()
            w1sb = sb(p3r, "w1sb", [128, CH, HID], BF16, side="right")
            w1ap = w1_h[0].rearrange("(kk p) h -> p kk h", p=128)
            t_w1d[0] = em.op("gpsimd", lambda g, w1ap=w1ap: g.dma_start(w1sb[:, :, :], w1ap),
                             waits=[t_qkproj_done], inc="sg1", amt=16)

        for h in range(H):
            par = h % 2
            p = par
            th = 32 * h
            tok_tr = {}
            for (src, pref, fo) in ((kfm, "k", 0), (vfm, "v", 128)):
                t_src = t_kev if pref == "k" else t_vev
                for c in range(4):
                    b = 0 if c < 2 else 64
                    cc4 = (c % 2) * 4
                    bk = ps_t0 if b == 0 else ps_t64
                    off = fo + (c % 2) * 64
                    tok_tr[(pref, c)] = em.op("tensor", lambda t, src=src, b=b, cc4=cc4, h=h, off=off, bk=bk: t.transpose(
                        bk[:, off:off + 64], src[b:b + 64, h, cc4:cc4 + 4, 0:32], idb[b:b + 64, b:b + 64]),
                        waits=tmax(t_src, t_idb, t0_attfm_rd if b == 0 else t64_evac_rd),
                        inc="sp" if (pref == "v" and c % 2 == 1) else None)
            t_kpe = {}
            t_vpe = {}
            for b, bk, ci in ((0, ps_t0, 0), (64, ps_t64, 2)):
                t_kpe[b] = em.op("vector", lambda v, par=par, ci=ci, bk=bk: v.tensor_copy(
                    kperm[:, par, ci:ci + 2, :], bk[:, 0:128].rearrange("p (c k) -> p c k", c=2)),
                    waits=tmax(tok_tr[("v", ci + 1)], slot_cons.get(("kperm", par))), inc="sv")
                t_vpe[b] = em.op("vector", lambda v, par=par, ci=ci, bk=bk: v.tensor_copy(
                    vperm[:, par, ci:ci + 2, 0:64], bk[:, 128:256].rearrange("p (c k) -> p c k", c=2)),
                    waits=tmax(tok_tr[("v", ci + 1)], slot_cons.get(("vperm", par)), t_vones), inc="sv")
            t0_evac_rd = tmax(t_kpe[0], t_vpe[0])
            t64_evac_rd = tmax(t_kpe[64], t_vpe[64])
            t_kperm = tmax(t_kpe[0], t_kpe[64])
            t_vperm = tmax(t_vpe[0], t_vpe[64])

            t_e0 = em.op("scalar", lambda a, par=par: a.activation(
                elut[:, par, :], kperm[:, par, :, :], AF.Relu, scale=-1.0),
                waits=tmax(t_kperm, slot_cons.get(("elut", par))), inc="sa")
            t_e1 = em.op("scalar", lambda a, par=par: a.activation(elue[:, par, :], elut[:, par, :], AF.Exp,
                                                                   scale=-1.0),
                         waits=tmax(t_e0, slot_cons.get(("elue", par))), inc="sa")
            t_skp = em.op("vector", lambda v, par=par: v.scalar_tensor_tensor(
                skperm[:, par, :, :],
                kperm[:, par, :, :], 0.0,
                elue[:, par, :], ALU.max, ALU.add),
                waits=tmax(t_e1, slot_cons.get(("skperm", par))), inc="sv")
            slot_cons[("kperm", par)] = t_skp

            t_S = {}

            def s_pair(tc_, h=h, par=par):
                b = 0 if tc_ < 2 else 64
                cc4 = (tc_ % 2) * 4
                bname = "s0" if b == 0 else "s64"
                bk = ps_s[0] if b == 0 else ps_s[1]
                rhs_even = qfm if b == 0 else (qfm if _noqsw else qsw)
                rhs_odd = (qfm if _noqsw else qsw) if b == 0 else qfm
                em.op("tensor", lambda t, b=b, cc4=cc4, h=h, bk=bk, re_=rhs_even: t.matmul(
                    bk[:, 0:256], kfm[b:b + 64, h, cc4:cc4 + 4, 0:32], re_[b:b + 64, h, 0:CH, 0:32],
                    start=True, stop=False, skip_group_check=True),
                    waits=tmax(t_kev, t_qev, t_qsw, cons[bname]))
                t_Smm = em.op("tensor", lambda t, b=b, cc4=cc4, h=h, bk=bk, ro=rhs_odd: t.matmul(
                    bk[:, 256:512], kfm[b:b + 64, h, cc4:cc4 + 4, 0:32], ro[b:b + 64, h, 0:CH, 0:32],
                    start=False, stop=True, skip_group_check=True), inc="sp")
                tE = em.op("scalar", lambda a, tc_=tc_, par=par, bk=bk: a.activation(
                    ethead[:, par, tc_, :], bk[:, :], AF.Exp, scale=0.125),
                    waits=tmax(t_Smm, slot_cons.get(("ethead", par))), inc="sa")
                t_S[tc_] = tE
                cons[bname] = tE
                return t_Smm

            s_pair(0)

            t_q0 = em.op("scalar", lambda a, par=par, h=h: a.activation(
                elut[:, par, :], qfm[:, h, :, :], AF.Relu, scale=-1.0),
                waits=tmax(t_qev, t_skp), inc="sa")
            t_q1 = em.op("scalar", lambda a, par=par: a.activation(elue[:, par, :], elut[:, par, :], AF.Exp,
                                                                   scale=-1.0),
                         waits=[t_q0], inc="sa")
            t_sqh = em.op("vector", lambda v, par=par, h=h: v.scalar_tensor_tensor(
                sqh[:, par, :, :],
                qfm[:, h, :, :], 0.0,
                elue[:, par, :], ALU.max, ALU.add),
                waits=tmax(t_q1, slot_cons.get(("sqh", par))), inc="sv")
            slot_cons[("elut", par)] = t_q1
            slot_cons[("elue", par)] = t_sqh
            t_qfm_readers = tmax(t_q0, t_sqh, t_qsw)

            t_qsw_free = s_pair(2)

            if pend_comb is not None:
                emit_combine(*pend_comb)
                pend_comb = None

            zpart = 0 if h < 8 else 32
            zoff = 64 * (h % 8)
            t_z = None
            for tc_ in range(4):
                t_z = em.op("tensor", lambda t, tc_=tc_, par=par, zpart=zpart, zoff=zoff,
                            fz=(s == 0 and h in (0, 8) and tc_ == 0),
                            lz=(s == NSEG - 1 and h == H - 1 and tc_ == 3): t.matmul(
                    ps_zat[zpart:zpart + 1, zoff:zoff + 64], onesc[:, :], skperm[:, par, tc_, :],
                    start=fz, stop=lz, skip_group_check=True),
                    waits=tmax(t_skp, zat_rd if tc_ == 0 else None),
                    inc="sp" if tc_ == 3 else None)

            t_zb = None
            if s > 0:
                t_rz = em.op("vector", lambda v, par=par, zpart=zpart, zoff=zoff: v.reciprocal(
                    rz[zpart:zpart + 1, par, :], ps_zat[zpart:zpart + 1, zoff:zoff + 64]),
                    waits=tmax(t_z, slot_cons.get(("rz", par))), inc="sv")
                zat_rd = t_rz
                t_gh0 = em.op("vector", lambda v, par=par, h=h, zpart=zpart: v.tensor_mul(
                    ghz[zpart:zpart + 1, par, :],
                    omgf[zpart:zpart + 1, 64 * h:64 * h + 64], rz[zpart:zpart + 1, par, :]),
                    waits=tmax(t_rz, t_omg, slot_cons.get(("ghz", par))), inc="sv")
                t_ghz = em.op("vector", lambda v, par=par, zpart=zpart: v.tensor_sub(
                    ghz[zpart:zpart + 1, par, :], rz[zpart:zpart + 1, par, :],
                    ghz[zpart:zpart + 1, par, :]),
                    waits=[t_gh0], inc="sv")
                slot_cons[("rz", par)] = t_ghz

            s_pair(1)
            s_pair(3)

            t_NU0 = t_NU = t_memmm = None
            for tc_ in (0, 1, 2, 3):
                t_memmm = em.op("tensor", lambda t, tc_=tc_, par=par, h=h,
                                fz=(s == 0 and h in (0, 1) and tc_ == 0),
                                lz=(s == NSEG - 1 and h == H - 1 and tc_ == 3): t.matmul(
                    _mem_ap(ps_mem, h), skperm[:, par, tc_, :], vperm[:, par, tc_, 0:64],
                    start=fz, stop=lz, skip_group_check=True),
                    waits=tmax(t_skp, t_vperm, cons["mem"] if tc_ == 0 else None),
                    inc="sp" if tc_ == 3 else None)
            if s > 0:
                for uc in (0, 1):
                    t_NU0 = em.op("tensor", lambda t, uc=uc, par=par, h=h: t.matmul(
                        ps_a0[:, 4 + uc, 0:65], sqh[0:64, par, 4 * uc:4 * uc + 4, :],
                        mem_sb[0:64, h, 0:65], start=True, stop=True, skip_group_check=True),
                        waits=tmax(t_sqh, t_memsb, a0_rd if uc == 0 else None), inc="sp")
                for uc in (2, 3):
                    t_NU = em.op("tensor", lambda t, uc=uc, par=par, h=h: t.matmul(
                        ps_a64[:, uc - 2, 0:65], sqh[64:128, par, 4 * (uc - 2):4 * (uc - 2) + 4, :],
                        mem_sb[64:128, h, 0:65], start=(uc == 2), stop=(uc == 3), skip_group_check=True),
                        waits=tmax(t_sqh, t_memsb, a64_rd if uc == 2 else None), inc="sp")
                t_zb = em.op("tensor", lambda t, par=par, zpart=zpart: t.matmul(
                    ps_a0[:, 6, 0:64], onesr[zpart:zpart + 1, :], ghz[zpart:zpart + 1, par, :],
                    start=True, stop=True,
                    skip_group_check=True), waits=tmax(t_ghz, a0_rd), inc="sp")
                slot_cons[("ghz", par)] = t_zb
            t_mem_upd_last = t_memmm
            t_nu_last = t_NU if s > 0 else t_memmm
            slot_cons[("skperm", par)] = t_memmm
            slot_cons[("sqh", par)] = t_NU if s > 0 else t_sqh

            if pend_up is not None:
                emit_up(*pend_up)
                pend_up = None

            t_AD = None
            first = True
            for tc_ in (0, 2, 1, 3):
                for uc in range(4):
                    t_AD = em.op("tensor", lambda t, tc_=tc_, uc=uc, par=par, st=first: t.matmul(
                        ps_a0[:, uc, 0:65], ethead[:, par, tc_, 128 * uc:128 * uc + 128], vperm[:, par, tc_, 0:65],
                        start=st, stop=(tc_ == 3 and uc == 3), skip_group_check=True),
                        waits=tmax(t_S[tc_], t_vperm, a0_rd if first else None),
                        inc="sp" if (tc_ == 3 and uc == 3) else None)
                    first = False
            slot_cons[("ethead", par)] = t_AD
            slot_cons[("vperm", par)] = tmax(t_memmm, t_AD)

            pend_comb = (par, h, s > 0, t_AD, t_NU0, t_NU, t_zb)
            if h == H - 1:
                emit_combine(*pend_comb)
                pend_comb = None
                emit_up(*pend_up)
                pend_up = None

        if s < NSEG - 1:
            t_me = None
            for h2 in range(H):
                t_me = em.op("vector", lambda v, h2=h2: v.tensor_copy(
                    mem_sb[0:64, h2, 0:64], _mem_ap(ps_mem, h2)),
                    waits=tmax(t_mem_upd_last, t_nu_last) if h2 == 0 else [t_mem_upd_last], inc="sv")
                t_me = em.op("vector", lambda v, h2=h2: v.tensor_copy(
                    mem_sb[64:128, h2, 0:64], _mem_ap(ps_mem, h2)), inc="sv")
            t_memsb = t_me
            cons["mem"] = t_me

        for dd in range(CH):
            bname = "s0" if dd % 2 == 0 else "s64"
            bk = ps_s[dd % 2]
            t_mm = None
            for ccp in range(CH):
                t_mm = em.op("tensor", lambda t, dd=dd, ccp=ccp, bk=bk: t.matmul(
                    bk[:, :], wosb[:, ccp, 128 * dd:128 * dd + 128], attfm[:, ccp, :],
                    start=(ccp == 0), stop=(ccp == CH - 1)),
                    waits=tmax(t_wo, t_att_ev_last, cons[bname]), inc="sp" if ccp == CH - 1 else None)
            t_wout[dd] = em.op("scalar", lambda a, dd=dd, s=s, bk=bk: a.copy(
                out_acc[:, dd, SEG * s:SEG * s + SEG], bk[:, :]), waits=[t_mm], inc="sa")
            cons[bname] = t_wout[dd]
            if dd == CH - 1:
                seg_tok["attfm_wout"] = t_mm
    t_attn_out = tmax(*t_wout)

    if debug == "attn":
        em.op("sync", lambda s: s.dma_start(dbg["oacc"].rearrange("p (c t) -> p c t", c=CH)[:, :, 0:SEG * NSEG],
                                            out_acc[:, :, 0:SEG * NSEG]),
              waits=[t_attn_out], inc="so", amt=16)
        em.op("sync", lambda s: s.nop(), waits=[("so", 16)])
        p2p.close(); p2.close()
        if p3r is not None:
            p3r.close()
        finish(nc, em, root)
        return nc

    t_psum2_free = tmax(a0_rd, a64_rd, zat_rd,
                        t0_evac_rd, t64_evac_rd, t0_attfm_rd,
                        cons["mem"])
    t_p2_free = tmax(t_attn_out, t_att_ev_last, t_mem_upd_last, t_qfm_readers,
                     cons["s0"], cons["s64"], t_psum2_free)

    p2p.close()
    p2.close()

    p3 = contextlib.ExitStack()
    w2sb = sb(p3, "w2sb", [128, 16, D], BF16)
    b1sb = sb(p3, "b1sb", [128, 16, E], F32)
    b2bf = sb(p3, "b2bf", [E, D], BF16)
    ohbf = sb(p3, "ohbf", [E, E, 128], BF16)
    h1g = sb(p3, "h1g", [128, 2, 16, SEG], BF16)
    h1t = sb(p3, "h1t", [128, 2, SEG], BF16)
    gbc = sb(p3, "gbc", [128, 4, SEG], BF16)
    p3p = contextlib.ExitStack()
    ps_m1 = [pt(p3p, "ps_m1a", [128, 512]), pt(p3p, "ps_m1b", [128, 512])]
    ps_y = [pt(p3p, "ps_y0", [128, 512]), pt(p3p, "ps_y1", [128, 512])]
    ps_gb = pt(p3p, "ps_gb", [128, 512])

    em.op("gpsimd", lambda g: g.dma_start(b2bf[:, :], b2_h), waits=[t_p2_free], inc="sgc", amt=16)
    em.op("gpsimd", lambda g: g.dma_start(ohbf[:, :, :], oh_h.rearrange("e (f p) -> e f p", f=E)),
          waits=[t_p2_free], inc="sgc", amt=16)
    t_b2oh = ("sgc", em.cnt["sgc"])
    t_b2 = t_oh = t_b2oh
    t_w2d[0] = em.op("gpsimd", lambda g: g.dma_start(
        w2sb[:, :, :], w2_h[0].rearrange("(hh p) d -> p hh d", p=128)),
        waits=[t_p2_free], inc="sg2", amt=16)
    t_b1 = em.op("sync", lambda s: s.dma_start(b1sb[:, :, :], b1t_h.rearrange("p (c e) -> p c e", c=16)),
                 waits=[t_p2_free], inc="ss", amt=16)
    t_pb_free = tmax(cons["s0"], cons["s64"], t_psum2_free)
    fcons = {"m1a": t_pb_free, "m1b": t_pb_free, "y0": t_pb_free, "y1": t_pb_free, "gb": t_pb_free}
    h1g_cons = [None, None]
    h1t_cons = [None, None]
    gbc_cons = [None] * 4
    t_oacc = {(dd, tt): t_attn_out for dd in range(CH) for tt in range(4)}
    t_gate = {}
    t_m2pe = {}

    for e in range(E):
        t_gbc = [None] * 4
        for tt in range(4):
            t_g = em.op("tensor", lambda t, e=e, tt=tt: t.matmul(
                ps_gb[:, :], ohbf[0:E, e, :], gates_fm[0:E, SEG * tt:SEG * tt + SEG],
                start=True, stop=True), waits=tmax(t_gfm, t_oh, fcons["gb"]), inc="sp")
            t_gbc[tt] = em.op("scalar", lambda a, tt=tt: a.copy(gbc[:, tt, :], ps_gb[:, :]),
                              waits=tmax(t_g, gbc_cons[tt]), inc="sa")
            fcons["gb"] = t_gbc[tt]
            gbc_cons[tt] = None

        def m1(tt, e=e, t_gbc=t_gbc):
            slot = tt % 2
            t_last_mm = None
            for hh in range(16):
                bname = "m1a" if hh % 2 == 0 else "m1b"
                bk = ps_m1[hh % 2]
                t_mm = None
                for kk in range(CH):
                    t_mm = em.op("tensor", lambda t, hh=hh, kk=kk, tt=tt, bk=bk: t.matmul(
                        bk[:, :], w1sb[:, kk, 128 * hh:128 * hh + 128], xn_bf[:, kk, SEG * tt:SEG * tt + SEG],
                        start=(kk == 0), stop=(kk == CH - 1)),
                        waits=tmax(t_w1d[e], t_xnbf, fcons[bname],
                                   h1g_cons[slot] if kk == 0 and hh == 0 else None),
                        inc="sp" if kk == CH - 1 else None)
                t_last_mm = t_mm
                t_r = em.op("scalar", lambda a, hh=hh, e=e, slot=slot, bk=bk: a.activation(
                    h1t[:, slot, :], bk[:, :], AF.Relu, bias=b1sb[:, hh, e:e + 1]),
                    waits=tmax(t_mm, t_b1, h1t_cons[slot]), inc="sa")
                t_gt = em.op("vector", lambda v, hh=hh, slot=slot, tt=tt: v.tensor_mul(
                    h1g[:, slot, hh, :], h1t[:, slot, :], gbc[:, tt, :]),
                    waits=tmax(t_r, t_gbc[tt]), inc="sv")
                h1t_cons[slot] = t_gt
                fcons[bname] = t_r
                t_gate[(e, tt)] = t_gt
            return t_last_mm

        def m2(tt, e=e):
            slot = tt % 2
            t_last_mm = None
            for dd in range(CH):
                bname = "y0" if dd % 2 == 0 else "y1"
                bk = ps_y[dd % 2]
                t_mm = None
                if e == 0:
                    em.op("tensor", lambda t, dd=dd, tt=tt, bk=bk: t.matmul(
                        bk[:, :], b2bf[0:E, 128 * dd:128 * dd + 128], gates_fm[0:E, SEG * tt:SEG * tt + SEG],
                        start=True, stop=False, skip_group_check=True),
                        waits=tmax(t_b2, t_gfm, fcons[bname]))
                for hh in range(16):
                    t_mm = em.op("tensor", lambda t, dd=dd, hh=hh, slot=slot, e=e, bk=bk: t.matmul(
                        bk[:, :], w2sb[:, hh, 128 * dd:128 * dd + 128], h1g[:, slot, hh, :],
                        start=(e > 0 and hh == 0), stop=(hh == 15), skip_group_check=True),
                        waits=tmax(t_w2d[e], t_gate[(e, tt)], fcons[bname]),
                        inc="sp" if hh == 15 else None)
                t_last_mm = t_mm
                t_ev = em.op("vector", lambda v, dd=dd, tt=tt, bk=bk: v.tensor_tensor(
                    out_acc[:, dd, SEG * tt:SEG * tt + SEG], out_acc[:, dd, SEG * tt:SEG * tt + SEG],
                    bk[:, :], ALU.add), waits=tmax(t_mm, t_oacc[(dd, tt)]), inc="sv")
                t_oacc[(dd, tt)] = t_ev
                fcons[bname] = t_ev
            t_m2pe[(e, tt)] = t_last_mm
            return t_last_mm

        sched = [(m1, 0), (m1, 1), (m2, 0), (m1, 2), (m2, 1), (m1, 3), (m2, 2), (m2, 3)]
        for fn, tt in sched:
            r = fn(tt)
            if fn is m2:
                h1g_cons[tt % 2] = r
        t_m1done[e] = t_gate[(e, 3)]
        t_m2done[e] = tmax(t_m2pe[(e, 2)], t_m2pe[(e, 3)], t_oacc[(CH - 1, 3)])
        if e + 1 < E:
            w1ap = w1_h[e + 1].rearrange("(kk p) h -> p kk h", p=128)
            w2ap = w2_h[e + 1].rearrange("(hh p) d -> p hh d", p=128)
            t_w1d[e + 1] = em.op("gpsimd", lambda g, w1ap=w1ap: g.dma_start(w1sb[:, :, :], w1ap),
                                 waits=[t_m1done[e]], inc="sg1", amt=16)
            t_w2d[e + 1] = em.op("gpsimd", lambda g, w2ap=w2ap: g.dma_start(w2sb[:, :, :], w2ap),
                                 waits=[t_m2done[e]], inc="sg2", amt=16)

    t_ffn = tmax(*[t_oacc[(dd, tt)] for dd in range(CH) for tt in range(4)])

    p3p.close()
    p3.close()
    p3r.close()

    p4r = contextlib.ExitStack()
    x2a = sb(p4r, "x2a", [128, 8, D], F32, side="right")
    p4 = contextlib.ExitStack()
    x2b = sb(p4, "x2b", [128, 8, D], F32)
    ybuf = sb(p4, "ybuf", [128, 2, D], F32)
    p4p = contextlib.ExitStack()
    ps_o = [pt(p4p, "ps_o0", [128, 1024], BF16), pt(p4p, "ps_o1", [128, 1024], BF16)]

    t_x2a = em.op("sync", lambda s: s.dma_start(x2a[:, :, :], x_ap[:, 0:8, :]),
                  waits=[t_m1done[E - 1]], inc="sx", amt=16)
    t_x2b = em.op("sync", lambda s: s.dma_start(x2b[:, :, :], x_ap[:, 8:16, :]),
                  waits=[t_m2done[E - 1]], inc="sq", amt=16)
    ocons = [None, None]
    ydma = [None, None]
    n_out = 0
    y_ap = y_h.rearrange("(tc p) d -> tc p d", p=128)
    for tc in range(TC):
        x2 = x2a if tc < 8 else x2b
        t_x2 = t_x2a if tc < 8 else t_x2b
        xtc = tc if tc < 8 else tc - 8
        bk = ps_o[tc % 2]
        t_tr = None
        for dd in range(CH):
            t_tr = em.op("tensor", lambda t, tc=tc, dd=dd, bk=bk: t.transpose(
                bk[:, 128 * dd:128 * dd + 128], out_acc[:, dd, 128 * tc:128 * tc + 128], idb[:, :]),
                waits=tmax(t_ffn, t_idb, ocons[tc % 2]), inc="sp" if dd == CH - 1 else None)
        t_add = em.op("vector", lambda v, tc=tc, xtc=xtc, x2=x2, bk=bk: v.tensor_tensor(
            ybuf[:, tc % 2, :], x2[:, xtc, :], bk[:, :], ALU.add),
            waits=tmax(t_tr, t_x2, ydma[tc % 2]), inc="sv")
        ocons[tc % 2] = t_add
        ydma[tc % 2] = em.op("sync", lambda s, tc=tc: s.dma_start(y_ap[tc], ybuf[:, tc % 2, :]),
                             waits=[t_add], inc=("so" if tc % 2 == 0 else "so2"), amt=16)
        n_out += 16
    em.op("sync", lambda s: s.nop(), waits=[("so", n_out // 2), ("so2", n_out // 2)])

    p4p.close()
    p4.close()
    p4r.close()
    finish(nc, em, root)
    return nc


def finish(nc, em, root):
    semstack = contextlib.ExitStack()
    sems = {s: semstack.enter_context(nc.semaphore(s)) for s in ("sp", "sv", "sa", "ss", "sx", "sq", "sgc", "sg1", "sg2", "so", "so2")}
    with nc.Block() as block:
        @block.tensor
        def _(t):
            em.replay_engine("tensor", t, sems)

        @block.vector
        def _(v):
            em.replay_engine("vector", v, sems)

        @block.scalar
        def _(a):
            em.replay_engine("scalar", a, sems)

        @block.gpsimd
        def _(g):
            em.replay_engine("gpsimd", g, sems)

        @block.sync
        def _(s):
            em.replay_engine("sync", s, sems)
    semstack.close()
    root.close()


def _layouts(inputs):
    f = np.float32

    def wt(W):
        return np.ascontiguousarray(np.asarray(W, f).T.reshape(CH, 128, D).transpose(1, 0, 2)).reshape(128, CH * D)

    betas = np.asarray(inputs["L0_attn_betas"], f).reshape(1, H * DK)
    return {
        "wqt": wt(inputs["L0_attn_Wq"]), "wkt": wt(inputs["L0_attn_Wk"]),
        "wvt": wt(inputs["L0_attn_Wv"]), "wot": wt(inputs["L0_attn_Wout"]),
        "betab": np.ascontiguousarray(np.broadcast_to(betas, (128, H * DK))),
        "wr": np.ascontiguousarray(np.asarray(inputs["L0_ffn_Wr"], f).reshape(CH, 128, E).transpose(1, 0, 2)).reshape(128, CH * E),
        "br": np.ascontiguousarray(np.asarray(inputs["L0_ffn_br"], f).reshape(E, 1)),
        "w1": np.ascontiguousarray(np.asarray(inputs["L0_ffn_W1"], f)),
        "b1t": np.ascontiguousarray(np.asarray(inputs["L0_ffn_b1"], f).reshape(E, 16, 128).transpose(2, 1, 0)).reshape(128, 16 * E),
        "w2": np.ascontiguousarray(np.asarray(inputs["L0_ffn_W2"], f)),
        "b2": np.ascontiguousarray(np.asarray(inputs["L0_ffn_b2"], f)),
        "ident": np.eye(128, dtype=f),
        "oh": np.ascontiguousarray(np.repeat(np.eye(E, dtype=f)[:, :, None], 128, axis=2)).reshape(E, E * 128),
    }


def kernel(**inputs):
    x = np.ascontiguousarray(np.asarray(inputs["L0_attn_x"], np.float32))
    shared = _layouts(inputs)
    nc = build()
    in_maps = []
    for i in range(NCORE):
        m = dict(shared)
        m["x"] = np.ascontiguousarray(x[i])
        in_maps.append(m)
    res = run_bass_kernel_spmd(nc, in_maps, core_ids=list(range(NCORE)))
    return np.stack([res.results[i]["y"] for i in range(NCORE)], axis=0).astype(np.float32)
